# version 08
# speedup vs baseline: 1.1051x; 1.0169x over previous
import numpy as np
import concourse.bass as bass
import concourse.mybir as mybir
from concourse.bass_utils import run_bass_kernel_spmd
from concourse.bass_core_rust import IndirectOffsetOnAxis

F32 = mybir.dt.float32
BF16 = mybir.dt.bfloat16
I32 = mybir.dt.int32
U32 = mybir.dt.uint32
AF = mybir.ActivationFunctionType
ALU = mybir.AluOpType
AX = mybir.AxisListType.X

D = 1024
NF = D // 128
NEG = -1.0e30


class Sem:
    def __init__(self, nc, name):
        self.h = nc.alloc_semaphore(name)
        self.val = 0

    def inc(self, n=1):
        self.val += n
        return self.val


class Prog:

    ENGS = ("sync", "tensor", "vector", "scalar", "gpsimd")
    CHAINED = ("vector", "scalar", "gpsimd")

    def __init__(self, nc):
        self.nc = nc
        self.ops = {e: [] for e in self.ENGS}
        self._sems = {}
        self.ms = {e: self.sem("MS_" + e) for e in self.ENGS}
        self.dirty = {e: False for e in self.ENGS}

    def sem(self, name):
        if name not in self._sems:
            self._sems[name] = Sem(self.nc, name)
        return self._sems[name]

    def wait(self, eng, sem, val):
        if val > 0:
            self.ops[eng].append(lambda e, s=sem.h, v=val: e.wait_ge(s, v))

    def dma(self, eng, out, in_, sem, **kw):
        v = sem.inc(16)
        self.ops[eng].append(
            lambda e, o=out, i=in_, s=sem.h, kw=kw: e.dma_start(out=o, in_=i, **kw).then_inc(s, 16))
        self.dirty[eng] = False
        return v

    def op(self, eng, fn, inc=None):
        if eng in self.CHAINED:
            m = self.ms[eng]
            assert inc is None or inc is m
            prev = m.val
            v = m.inc(1)
            self.ops[eng].append(
                lambda e, fn=fn, s=m.h, v=v, pv=prev: (
                    e.wait_ge(s, pv), fn(e).then_inc(s, 1))[1] if pv > 0
                else fn(e).then_inc(s, 1))
            return v
        if inc is not None:
            v = inc.inc(1)
            self.ops[eng].append(
                lambda e, fn=fn, s=inc.h, v=v: fn(e).then_inc(s, 1))
            self.dirty[eng] = False
            return v
        self.ops[eng].append(fn)
        self.dirty[eng] = True
        return None

    def barrier(self, extra_sems=()):
        for e in self.ENGS:
            if self.dirty[e]:
                m = self.ms[e]
                v = m.inc(1)
                self.ops[e].append(lambda eng, s=m.h, v=v: eng.nop().then_inc(s, 1))
                self.dirty[e] = False
        if not hasattr(self, "_bsnap"):
            self._bsnap = {}
        vals = [(s, s.val) for s in self._sems.values()
                if s.val > 0 and self._bsnap.get(id(s)) != s.val]
        for s, v in vals:
            self._bsnap[id(s)] = v
        for e in self.ENGS:
            for s, v in vals:
                self.wait(e, s, v)

    def replay(self, block):
        for name in self.ENGS:
            lst = self.ops[name]
            getattr(block, name)(lambda eng, lst=lst: [f(eng) for f in lst])


class Arena:

    def __init__(self, t, cols):
        self.t = t
        self.cols = cols
        self.off = 0

    def take(self, cols):
        assert self.off + cols <= self.cols, (self.off, cols, self.cols)
        ap = self.t[:, self.off:self.off + cols]
        self.off += cols
        return ap

    def mark(self):
        return self.off

    def reset(self, to=0):
        self.off = to



class Bundle:

    AFCOLS = 17920
    ABCOLS = 61440

    def __init__(self, T=4096):
        from contextlib import ExitStack
        nc = bass.Bass("TRN2", target_bir_lowering=False)
        self.nc = nc
        self.T = T
        self._st = ExitStack()
        ent = self._st.enter_context
        self.AFt = ent(nc.sbuf_tensor("AF0", [128, self.AFCOLS], F32))
        self.ABt = ent(nc.sbuf_tensor("AB0", [128, self.ABCOLS], BF16))
        self.AF = Arena(self.AFt, self.AFCOLS)
        self.AB = Arena(self.ABt, self.ABCOLS)
        self.pA = ent(nc.psum_tensor("pA", [128, 512], F32))
        self.pB = ent(nc.psum_tensor("pB", [128, 512], F32))
        self.pC = ent(nc.psum_tensor("pC", [128, 512], F32))
        self.pD = ent(nc.psum_tensor("pD", [128, 512], F32))
        self.pE = ent(nc.psum_tensor("pE", [128, 512], F32))
        self.pF = ent(nc.psum_tensor("pF", [128, 512], F32))
        self.pG = ent(nc.psum_tensor("pG", [128, 512], F32))
        self.pT = ent(nc.psum_tensor("pT", [128, 128], BF16))
        self.conesf = ent(nc.sbuf_tensor("conesf", [128, 128], F32))
        self.cidf = ent(nc.sbuf_tensor("cidf", [128, 128], F32))
        self.cidb = ent(nc.sbuf_tensor("cidb", [128, 128], BF16))
        self.epsc = ent(nc.sbuf_tensor("epsc", [128, 4], F32))
        self.p = Prog(nc)
        onesf_d = nc.dram_tensor("onesf", [128, 128], F32, kind="ExternalInput").ap()
        idf_d = nc.dram_tensor("id128f", [128, 128], F32, kind="ExternalInput").ap()
        idb_d = nc.dram_tensor("id128b", [128, 128], BF16, kind="ExternalInput").ap()
        CLD = self.p.sem("CLD")
        self.p.dma("sync", self.conesf[:], onesf_d, CLD)
        self.p.dma("sync", self.cidf[:], idf_d, CLD)
        self.p.dma("sync", self.cidb[:], idb_d, CLD)
        self.p.op("vector", lambda e: e.memset(self.epsc[:, 0:1], 1e-5))
        self.p.barrier()

    def ded(self, name, shape, dtype):
        return self._st.enter_context(self.nc.sbuf_tensor(name, shape, dtype))

    def sbuf(self, rows, cols, dtype):
        a = self.AF if dtype is F32 else self.AB
        ap = a.take(cols)
        return ap if rows == 128 else ap[0:rows, :]

    def finish(self):
        self.p.wait("sync", self.p.sem("SOF"), 0)
        block = self._st.enter_context(self.nc.Block())
        self.p.replay(block)
        self._st.close()
        return self.nc


def bundle_const_inputs():
    import ml_dtypes
    return {
        "onesf": np.ones((128, 128), np.float32),
        "id128f": np.eye(128, dtype=np.float32),
        "id128b": np.eye(128).astype(ml_dtypes.bfloat16),
    }

def ln_chunk(p, xbuf, sqa, meanb, rstdb, vtmp, conesf, epscol, pS, pQ, MM, EV, EA,
             nf=NF, dmodel=D):
    sq_done = p.op("scalar", lambda e: e.activation(sqa[:, 0:512 * nf], xbuf[:, 0:512 * nf], AF.Square), EA)
    p.wait("tensor", EA, sq_done)
    for f in range(nf):
        p.op("tensor", lambda e, f=f: e.matmul(
            pS[:], conesf[:], xbuf[:, 512 * f:512 * (f + 1)],
            start=(f == 0), stop=(f == nf - 1)))
    mm_done = None
    for f in range(nf):
        mm_done = p.op("tensor", lambda e, f=f: e.matmul(
            pQ[:], conesf[:], sqa[:, 512 * f:512 * (f + 1)],
            start=(f == 0), stop=(f == nf - 1)), inc=(MM if f == nf - 1 else None))
    p.wait("scalar", MM, mm_done)
    m_done = p.op("scalar", lambda e: e.activation(meanb[:], pS[:], AF.Copy, scale=1.0 / dmodel), EA)
    p.wait("scalar", EA, m_done)
    msq = p.op("scalar", lambda e: e.activation(vtmp[:], meanb[:], AF.Square), EA)
    p.wait("vector", EA, msq)
    var_done = p.op("vector", lambda e: e.scalar_tensor_tensor(
        vtmp[:], pQ[:], 1.0 / dmodel, vtmp[:], ALU.mult, ALU.subtract), EV)
    p.wait("scalar", EV, var_done)
    s_done = p.op("scalar", lambda e: e.activation(vtmp[:], vtmp[:], AF.Sqrt, bias=epscol), EA)
    p.wait("vector", EA, s_done)
    r_done = p.op("vector", lambda e: e.reciprocal(rstdb[:], vtmp[:]), EV)
    return r_done



def emit_l0_attn(B, T, pref, xT, x1T):
    nc = B.nc
    p = B.p
    conesf = B.conesf
    afm = B.AF.mark()
    abm = B.AB.mark()
    NT = T // 512
    NB = T // 128
    H = 16
    slopes = [2.0 ** (-0.5 * (h + 1)) for h in range(H)]

    Wq = nc.dram_tensor(pref + "WqT", [D, D], F32, kind="ExternalInput").ap()
    Wk = nc.dram_tensor(pref + "WkT", [D, D], F32, kind="ExternalInput").ap()
    Wv = nc.dram_tensor(pref + "WvT", [D, D], F32, kind="ExternalInput").ap()
    Wo = nc.dram_tensor(pref + "WoT", [D, D], F32, kind="ExternalInput").ap()
    bcols_d = nc.dram_tensor(pref + "bcols", [128, 32], F32, kind="ExternalInput").ap()
    bvrow_d = nc.dram_tensor(pref + "bvrow", [1, D], F32, kind="ExternalInput").ap()
    biask_d = nc.dram_tensor(pref + "biask", [66, 128], BF16, kind="ExternalInput").ap()
    alibi_d = nc.dram_tensor(pref + "alibirhs", [66, 8 * 512], BF16, kind="ExternalInput").ap()
    cmask_d = nc.dram_tensor(pref + "cmask", [128, 4 * 512], BF16, kind="ExternalInput").ap()
    biacol_d = nc.dram_tensor(pref + "biacol", [128, 576], F32, kind="ExternalInput").ap()
    qTd = nc.dram_tensor(pref + "qTd", [D, T], BF16, kind="Internal").ap()
    kTd = nc.dram_tensor(pref + "kTd", [D, T], BF16, kind="Internal").ap()
    aoTd = nc.dram_tensor(pref + "aoTd", [D, T], BF16, kind="Internal").ap()
    vtokd = nc.dram_tensor(pref + "vtokd", [T // 128, 128, 1024], BF16, kind="Internal").ap()

    if True:
        vsl = B.sbuf(128, 2 * 130, BF16)
        vstage = B.sbuf(128, 2 * 512, BF16)
        cbiask = B.sbuf(66, 128, BF16)
        calibi = B.sbuf(66, 8 * 512, BF16)
        cmaskt = B.sbuf(128, 4 * 512, BF16)
        biacol = B.sbuf(128, 576, F32)
        bcolsb = B.sbuf(128, 32, F32)
        bvrow = B.sbuf(1, D, BF16)
        xa = B.sbuf(128, 4096, F32)
        xb_ = B.sbuf(128, 4096, F32)
        sqa = B.sbuf(128, 4096, F32)
        xnc = B.sbuf(128, 4096, BF16)
        wq_sb = B.sbuf(128, 8 * 1024, BF16)
        wk_sb = B.sbuf(128, 8 * 1024, BF16)
        wv_sb = B.sbuf(128, 8 * 1024, BF16)
        meanb = B.sbuf(128, 512, F32)
        rstdb = B.sbuf(128, 512, F32)
        vtmp = B.sbuf(128, 512, F32)
        qevac = B.sbuf(128, 2 * 512, BF16)
        kevac = B.sbuf(128, 2 * 512, BF16)
        qsl = B.sbuf(128, 2 * 512, BF16)
        ksl = B.sbuf(128, 2 * 128, BF16)
        attA = B.sbuf(128, 2 * 512, BF16)
        attB = B.sbuf(128, 2 * 512, BF16)
        mtmp = B.sbuf(128, 2 * 512, F32)
        rbc = B.sbuf(64, 2 * 512, F32)
        aosl = B.sbuf(64, 4 * 512, BF16)
        rcp = B.sbuf(128, 512, F32)

        pA, pB, pC, pD, pE, pF = B.pA, B.pB, B.pC, B.pD, B.pE, B.pF
        LD = p.sem(pref + "LD")
        WD = p.sem(pref + "WD")
        QS = p.sem(pref + "QS")
        VS = p.sem(pref + "VS")
        MM = p.sem(pref + "MM")
        EV = p.ms["vector"]
        EA = p.ms["scalar"]
        SO = p.sem(pref + "SO")

        p.dma("sync", cbiask[:], biask_d, LD)
        p.dma("sync", calibi[:], alibi_d, LD)
        p.dma("sync", cmaskt[:], cmask_d, LD)
        p.dma("sync", biacol[:], biacol_d, LD)
        p.dma("sync", bcolsb[:], bcols_d, LD)
        p.dma("gpsimd", bvrow[:], bvrow_d, WD)
        p.op("vector", lambda e: e.memset(vsl[:], 1.0), EV)
        p.barrier()

        for c in range(NT):
            lv = [p.dma("sync", xa[:, 512 * f:512 * (f + 1)],
                        xT[128 * f:128 * (f + 1), 512 * c:512 * (c + 1)], LD)
                  for f in range(NF)]
            wls = []
            for (wsb, wdr) in ((wq_sb, Wq), (wk_sb, Wk), (wv_sb, Wv)):
                for kf in range(NF):
                    wls.append(p.dma("gpsimd", wsb[:, 1024 * kf:1024 * (kf + 1)],
                                     wdr[128 * kf:128 * (kf + 1), :], WD))
            p.wait("scalar", LD, lv[-1])
            p.wait("tensor", LD, lv[-1])
            r_done = ln_chunk(p, xa, sqa, meanb, rstdb, vtmp, conesf, bcolsb[:, 16:17], pA, pB, MM, EV, EA)
            p.wait("vector", EV, r_done)
            for f in range(NF):
                p.op("vector", lambda e, f=f: e.tensor_sub(
                    sqa[:, 512 * f:512 * (f + 1)], xa[:, 512 * f:512 * (f + 1)], meanb[:]))
            a_done = None
            for f in range(NF):
                a_done = p.op("vector", lambda e, f=f: e.tensor_mul(
                    xnc[:, 512 * f:512 * (f + 1)], sqa[:, 512 * f:512 * (f + 1)], rstdb[:]),
                    inc=(EV if f == NF - 1 else None))
            p.wait("tensor", EV, a_done)
            p.wait("tensor", WD, wls[-1])
            evac_ea = {}
            evac_qs = {}
            for (wi, wsb, dst, evb, bofs) in ((0, wq_sb, qTd, qevac, 0), (1, wk_sb, kTd, kevac, 8)):
                for et in range(NF):
                    g = wi * NF + et
                    bank = pC if g % 2 == 0 else pD
                    if g >= 2:
                        p.wait("tensor", EA, evac_ea[g - 2])
                    mmv = None
                    for kf in range(NF):
                        mmv = p.op("tensor", lambda e, wsb=wsb, et=et, kf=kf, bank=bank: e.matmul(
                            bank[:], wsb[:, 1024 * kf + 128 * et:1024 * kf + 128 * (et + 1)],
                            xnc[:, 512 * kf:512 * (kf + 1)],
                            start=(kf == 0), stop=(kf == NF - 1)),
                            inc=(MM if kf == NF - 1 else None))
                    eb = evb[:, 512 * (et % 2):512 * (et % 2 + 1)]
                    p.wait("scalar", MM, mmv)
                    if g >= 2:
                        p.wait("scalar", QS, QS.val)
                    ev = p.op("scalar", lambda e, bank=bank, eb=eb, et=et, bofs=bofs: e.activation(
                        eb, bank[:], AF.Identity, bias=bcolsb[:, bofs + et:bofs + et + 1]), EA)
                    evac_ea[g] = ev
                    p.wait("scalar", EA, ev)
                    qsv = p.dma("scalar", dst[128 * et:128 * (et + 1), 512 * c:512 * (c + 1)], eb, QS)
                    evac_qs[g] = qsv
            vev = {}
            for j in range(4):
                tau = 4 * c + j
                for ec in range(2):
                    g = 2 * j + ec
                    bank = pE if g % 2 == 0 else pF
                    if g >= 2:
                        p.wait("tensor", EV, vev[g - 2])
                    for kf in range(NF):
                        p.op("tensor", lambda e, j=j, ec=ec, kf=kf, bank=bank: e.matmul(
                            bank[:], xnc[:, 512 * kf + 128 * j:512 * kf + 128 * (j + 1)],
                            wv_sb[:, 1024 * kf + 512 * ec:1024 * kf + 512 * (ec + 1)],
                            start=(kf == 0), stop=False))
                    mmv = p.op("tensor", lambda e, ec=ec, bank=bank: e.matmul(
                        bank[:], cbiask[0:1, :], bvrow[0:1, 512 * ec:512 * (ec + 1)],
                        start=False, stop=True), inc=MM)
                    p.wait("vector", MM, mmv)
                    vst = vstage[:, 512 * (g % 2):512 * (g % 2 + 1)]
                    if g >= 2:
                        p.wait("vector", VS, VS.val)
                    vev[g] = p.op("vector", lambda e, vst=vst, bank=bank: e.tensor_copy(vst, bank[:]), EV)
                    p.wait("gpsimd", EV, vev[g])
                    p.dma("gpsimd", vtokd[tau, :, 512 * ec:512 * (ec + 1)], vst, VS)
            p.barrier()

        QL = p.sem(pref + "QL")
        KD0 = p.sem(pref + "KD0")
        KD1 = p.sem(pref + "KD1")
        VD0 = p.sem(pref + "VD0")
        VD1 = p.sem(pref + "VD1")
        QSp = [p.sem(pref + "QS0"), p.sem(pref + "QS1")]
        qk_guard = {0: 0, 1: 0}
        slot_guard = {0: 0, 1: 0}
        bank_cons = {0: [], 1: []}
        lastexp_g = 0
        md_prev = 0
        qs_out = {0: 0, 1: 0}
        itn = 0
        for i in range(8):
            sA, sB = slopes[2 * i], slopes[2 * i + 1]
            for c in range(NT):
                nk = 4 * c + 4
                ipar = itn % 2
                itn += 1
                qslb = qsl[:, 512 * ipar:512 * (ipar + 1)]
                aoslb = aosl[:, 1024 * ipar:1024 * (ipar + 1)]
                p.wait("sync", MM, qk_guard[ipar])
                qv = p.dma("sync", qslb, qTd[128 * i:128 * (i + 1), 512 * c:512 * (c + 1)], QL)
                kvs = {}
                exp_done = {}
                av_done = {}
                for kt in range(nk):
                    par = kt % 2
                    kb = ksl[:, 128 * par:128 * (par + 1)]
                    kdsem = KD0 if par == 0 else KD1
                    p.wait("sync", MM, slot_guard[par])
                    kvs[kt] = p.dma("sync", kb, kTd[128 * i:128 * (i + 1), 128 * kt:128 * (kt + 1)], kdsem)
                    vdsem = VD0 if par == 0 else VD1
                    vdst = vsl[:, 130 * par:130 * par + 130].rearrange(
                        "q (g w) -> q g w", w=65)[:, :, 0:64]
                    vsrc = vtokd[kt, :, 128 * i:128 * (i + 1)].rearrange("q (g w) -> q g w", g=2)
                    vv = p.dma("sync", vdst, vsrc, vdsem)
                    SA = pA if par == 0 else pB
                    SB = pC if par == 0 else pD
                    p.wait("tensor", kdsem, kvs[kt])
                    if kt == 0:
                        p.wait("tensor", QL, qv)
                    for (sem2, val2) in bank_cons[par]:
                        p.wait("tensor", sem2, val2)
                    qk = None
                    p.op("tensor", lambda e, kb=kb, SA=SA, qslb=qslb: e.matmul(
                        SA[:], kb[0:64, :], qslb[0:64, :], start=True, stop=False))
                    p.op("tensor", lambda e, kb=kb, SB=SB, qslb=qslb: e.matmul(
                        SB[:], kb[64:128, :], qslb[64:128, :], start=True, stop=False,
                        tile_position=(64, 0)))
                    p.op("tensor", lambda e, i=i, SA=SA: e.matmul(
                        SA[:], cbiask[0:2, :], calibi[0:2, 512 * i:512 * (i + 1)],
                        start=False, stop=True))
                    qk = p.op("tensor", lambda e, i=i, SB=SB: e.matmul(
                        SB[:], cbiask[64:66, :], calibi[64:66, 512 * i:512 * (i + 1)],
                        start=False, stop=True, tile_position=(64, 0)), inc=MM)
                    diag = kt - 4 * c
                    aA = attA[:, 512 * par:512 * (par + 1)]
                    aB = attB[:, 512 * par:512 * (par + 1)]
                    dlt = 4 * c - kt + 3
                    biA = biacol[:, 36 * (2 * i) + dlt:36 * (2 * i) + dlt + 1]
                    biB = biacol[:, 36 * (2 * i + 1) + dlt:36 * (2 * i + 1) + dlt + 1]
                    if 0 <= diag:
                        p.wait("vector", MM, qk)
                        p.wait("vector", EA, lastexp_g)
                        p.op("vector", lambda e, SA=SA, diag=diag: e.scalar_tensor_tensor(
                            mtmp[:, 0:512], SA[:], 1.0, cmaskt[:, 512 * diag:512 * (diag + 1)],
                            ALU.mult, ALU.add))
                        mk = p.op("vector", lambda e, SB=SB, diag=diag: e.scalar_tensor_tensor(
                            mtmp[:, 512:1024], SB[:], 1.0, cmaskt[:, 512 * diag:512 * (diag + 1)],
                            ALU.mult, ALU.add), inc=EV)
                        bank_cons[par] = [(EV, mk)]
                        p.wait("scalar", EV, mk)
                        p.wait("scalar", MM, slot_guard[par])
                        p.op("scalar", lambda e, aA=aA, biA=biA: e.activation(
                            aA, mtmp[:, 0:512], AF.Exp, scale=0.125, bias=biA))
                        exp_done[kt] = p.op("scalar", lambda e, aB=aB, biB=biB: e.activation(
                            aB, mtmp[:, 512:1024], AF.Exp, scale=0.125, bias=biB), inc=EA)
                    else:
                        p.wait("scalar", MM, qk)
                        p.wait("scalar", MM, slot_guard[par])
                        p.op("scalar", lambda e, SA=SA, aA=aA, biA=biA: e.activation(
                            aA, SA[:], AF.Exp, scale=0.125, bias=biA))
                        exp_done[kt] = p.op("scalar", lambda e, SB=SB, aB=aB, biB=biB: e.activation(
                            aB, SB[:], AF.Exp, scale=0.125, bias=biB), inc=EA)
                        bank_cons[par] = [(EA, exp_done[kt])]
                    lastexp_g = exp_done[kt]
                    p.wait("tensor", EA, exp_done[kt])
                    p.wait("tensor", vdsem, vv)
                    if kt == 0 and md_prev:
                        p.wait("tensor", EV, md_prev)
                    vA = vsl[:, 130 * par:130 * par + 65]
                    vB = vsl[:, 130 * par + 65:130 * par + 130]
                    p.op("tensor", lambda e, kt=kt, vA=vA, aA=aA: e.matmul(
                        pE[0:65, :], vA, aA, start=(kt == 0), stop=(kt == nk - 1)))
                    av_done[kt] = p.op("tensor", lambda e, kt=kt, vB=vB, aB=aB: e.matmul(
                        pF[0:65, :], vB, aB, start=(kt == 0), stop=(kt == nk - 1)), inc=MM)
                    slot_guard[par] = av_done[kt]
                qk_guard[ipar] = av_done[nk - 1]
                p.wait("vector", MM, av_done[nk - 1])
                p.op("vector", lambda e: e.reciprocal(rcp[0:1, :], pE[64:65, :]))
                rv = p.op("vector", lambda e: e.reciprocal(rcp[64:65, :], pF[64:65, :]), inc=EV)
                p.wait("tensor", EV, rv)
                p.op("tensor", lambda e: e.matmul(
                    pA[0:64, :], conesf[0:1, 0:64], rcp[0:1, :], start=True, stop=True))
                bv = p.op("tensor", lambda e: e.matmul(
                    pB[0:64, :], conesf[64:65, 0:64], rcp[64:65, :], start=True, stop=True), inc=MM)
                p.wait("vector", MM, bv)
                p.wait("vector", QSp[ipar], qs_out[ipar])
                p.op("vector", lambda e: e.tensor_copy(rbc[:, 0:512], pA[0:64, :]))
                rb2 = p.op("vector", lambda e: e.tensor_copy(rbc[:, 512:1024], pB[0:64, :]), inc=EV)
                p.op("vector", lambda e, aoslb=aoslb: e.tensor_mul(aoslb[:, 0:512], pE[0:64, :], rbc[:, 0:512]))
                md = p.op("vector", lambda e, aoslb=aoslb: e.tensor_mul(
                    aoslb[:, 512:1024], pF[0:64, :], rbc[:, 512:1024]), inc=EV)
                md_prev = md
                bank_cons[0] = bank_cons[0] + [(EV, rb2)]
                bank_cons[1] = bank_cons[1] + [(EV, rb2)]
                p.wait("scalar", EV, md)
                p.dma("scalar", aoTd[128 * i:128 * i + 64, 512 * c:512 * (c + 1)], aoslb[:, 0:512], QSp[ipar])
                qs_out[ipar] = p.dma("scalar", aoTd[128 * i + 64:128 * i + 128, 512 * c:512 * (c + 1)],
                                     aoslb[:, 512:1024], QSp[ipar])
        p.barrier()

        for c in range(NT):
            av = [p.dma("sync", xnc[:, 512 * f:512 * (f + 1)],
                        aoTd[128 * f:128 * (f + 1), 512 * c:512 * (c + 1)], LD)
                  for f in range(NF)]
            wls = [p.dma("gpsimd", wq_sb[:, 1024 * kf:1024 * (kf + 1)],
                         Wo[128 * kf:128 * (kf + 1), :], WD) for kf in range(NF)]
            xv = [p.dma("sync", xb_[:, 512 * f:512 * (f + 1)],
                        xT[128 * f:128 * (f + 1), 512 * c:512 * (c + 1)], LD)
                  for f in range(NF)]
            p.wait("tensor", LD, xv[-1])
            p.wait("tensor", WD, wls[-1])
            wo_ea = {}
            for dt in range(NF):
                bank = pC if dt % 2 == 0 else pD
                if dt >= 2:
                    p.wait("tensor", EA, wo_ea[dt - 2])
                mmv = None
                for kf in range(NF):
                    mmv = p.op("tensor", lambda e, dt=dt, kf=kf, bank=bank: e.matmul(
                        bank[:], wq_sb[:, 1024 * kf + 128 * dt:1024 * kf + 128 * (dt + 1)],
                        xnc[:, 512 * kf:512 * (kf + 1)],
                        start=(kf == 0), stop=(kf == NF - 1)),
                        inc=(MM if kf == NF - 1 else None))
                p.wait("scalar", MM, mmv)
                ev = p.op("scalar", lambda e, bank=bank, dt=dt: e.activation(
                    xa[:, 512 * dt:512 * (dt + 1)], bank[:], AF.Identity,
                    bias=bcolsb[:, 24 + dt:24 + dt + 1]), EA)
                wo_ea[dt] = ev
            p.wait("scalar", EA, ev)
            p.wait("tensor", EA, ev)
            r_done = ln_chunk(p, xa, sqa, meanb, rstdb, vtmp, conesf, bcolsb[:, 16:17], pA, pB, MM, EV, EA)
            p.wait("vector", EV, r_done)
            p.wait("vector", LD, xv[-1])
            for f in range(NF):
                p.op("vector", lambda e, f=f: e.tensor_sub(
                    sqa[:, 512 * f:512 * (f + 1)], xa[:, 512 * f:512 * (f + 1)], meanb[:]))
            for f in range(NF):
                p.op("vector", lambda e, f=f: e.tensor_mul(
                    sqa[:, 512 * f:512 * (f + 1)], sqa[:, 512 * f:512 * (f + 1)], rstdb[:]))
            od = None
            for f in range(NF):
                od = p.op("vector", lambda e, f=f: e.tensor_add(
                    xb_[:, 512 * f:512 * (f + 1)], xb_[:, 512 * f:512 * (f + 1)],
                    sqa[:, 512 * f:512 * (f + 1)]), inc=(EV if f == NF - 1 else None))
            p.wait("sync", EV, od)
            for f in range(NF):
                p.dma("sync", x1T[128 * f:128 * (f + 1), 512 * c:512 * (c + 1)],
                      xb_[:, 512 * f:512 * (f + 1)], SO)
            p.barrier()

        p.wait("sync", SO, SO.val)
        p.barrier()
        B.AF.reset(afm)
        B.AB.reset(abm)





def build_l0_attn(T=4096):
    B = Bundle(T=T)
    xT = B.nc.dram_tensor("xT", [D, T], F32, kind="ExternalInput").ap()
    x1T = B.nc.dram_tensor("x1T", [D, T], F32, kind="ExternalOutput").ap()
    emit_l0_attn(B, T, "", xT, x1T)
    return B.finish()

def rms_chunk(p, xbuf, sqa, rmsb, conesf, epscol, pS, MM, EV, EA, nf=NF, dmodel=D):
    sq_done = p.op("scalar", lambda e: e.activation(sqa[:, 0:512 * nf], xbuf[:, 0:512 * nf], AF.Square), EA)
    p.wait("tensor", EA, sq_done)
    mm_done = None
    for f in range(nf):
        mm_done = p.op("tensor", lambda e, f=f: e.matmul(
            pS[:], conesf[:], sqa[:, 512 * f:512 * (f + 1)],
            start=(f == 0), stop=(f == nf - 1)), inc=(MM if f == nf - 1 else None))
    p.wait("scalar", MM, mm_done)
    s_done = p.op("scalar", lambda e: e.activation(sqa[:, 0:512], pS[:], AF.Sqrt,
                                                   scale=1.0 / dmodel, bias=epscol), EA)
    p.wait("vector", EA, s_done)
    r_done = p.op("vector", lambda e: e.reciprocal(rmsb[:], sqa[:, 0:512]), EV)
    return r_done


def emit_l0_ffn(B, T, pref, xT, x2T):
    nc = B.nc
    p = B.p
    conesf = B.conesf
    afm0 = B.AF.mark()
    abm0 = B.AB.mark()
    NT = T // 512
    NB = T // 128
    NQ = max(1, T // 1024)
    QT = T // NQ
    QF = QT // 128
    TCW = min(512, QT)
    NTC = QT // TCW
    TTS = TCW // 128
    E = 32

    qkvw_d = nc.dram_tensor(pref + "qkvw", [D, 1536], F32, kind="ExternalInput").ap()
    projw_d = nc.dram_tensor(pref + "projw", [D, D], F32, kind="ExternalInput").ap()
    gatew_d = nc.dram_tensor(pref + "gatew", [D, 32], F32, kind="ExternalInput").ap()
    wup_d = nc.dram_tensor(pref + "wup", [E, D, 2048], F32, kind="ExternalInput").ap()
    wdn_d = nc.dram_tensor(pref + "wdn", [E, D, D], F32, kind="ExternalInput").ap()
    ncols_d = nc.dram_tensor(pref + "ncols", [128, 32], F32, kind="ExternalInput").ap()
    qkvb_d = nc.dram_tensor(pref + "qkvb", [128, 10], F32, kind="ExternalInput").ap()
    vbrow_d = nc.dram_tensor(pref + "vbrow", [1, 256], F32, kind="ExternalInput").ap()
    gateb_d = nc.dram_tensor(pref + "gateb", [32, 1], F32, kind="ExternalInput").ap()
    sink_d = nc.dram_tensor(pref + "sinkr", [1, 16], F32, kind="ExternalInput").ap()
    onesb_d = nc.dram_tensor(pref + "onesb", [1, 128], BF16, kind="ExternalInput").ap()
    wmask_d = nc.dram_tensor(pref + "wmask", [128, 5 * 512], BF16, kind="ExternalInput").ap()
    ctabq_d = nc.dram_tensor(pref + "ctabq", [128, T], BF16, kind="ExternalInput").ap()
    stabq_d = nc.dram_tensor(pref + "stabq", [128, T], BF16, kind="ExternalInput").ap()
    ctabk_d = nc.dram_tensor(pref + "ctabk", [128, T], BF16, kind="ExternalInput").ap()
    stabk_d = nc.dram_tensor(pref + "stabk", [128, T], BF16, kind="ExternalInput").ap()
    iota32_d = nc.dram_tensor(pref + "iota32", [128, 32], F32, kind="ExternalInput").ap()
    xinTd = nc.dram_tensor(pref + "xinTd", [D, T], F32, kind="Internal").ap()
    qTd = nc.dram_tensor(pref + "qTd2", [D, T], BF16, kind="Internal").ap()
    kTd = nc.dram_tensor(pref + "kTd2", [256, T], BF16, kind="Internal").ap()
    vtokd = nc.dram_tensor(pref + "vtokd2", [NB, 128, 256], BF16, kind="Internal").ap()
    aoTd = nc.dram_tensor(pref + "aoTd2", [D, T], BF16, kind="Internal").ap()
    xpTd = nc.dram_tensor(pref + "xpTd", [D, T], F32, kind="Internal").ap()
    h2Td = nc.dram_tensor(pref + "h2Td", [D, T], BF16, kind="Internal").ap()
    y2Td = nc.dram_tensor(pref + "y2Td", [D, T], F32, kind="Internal").ap()

    if True:
        cwm = B.sbuf(128, 5 * 512, BF16)
        ncols = B.sbuf(128, 32, F32)
        qkvb = B.sbuf(128, 10, F32)
        vbrow = B.sbuf(1, 256, BF16)
        gateb = B.sbuf(32, 1, F32)
        sinkb = B.sbuf(128, 16, F32)
        conesb = B.sbuf(1, 128, BF16)
        ciota = B.sbuf(128, 32, F32)
        gltok = B.sbuf(128, NB * 32, F32)
        comb = B.sbuf(128, NB * 32, F32)
        idxu = B.ded(pref + "idxu", [128, 8], U32)
        idxf = B.sbuf(128, 8, F32)
        m8 = B.sbuf(128, 8, F32)
        w4 = B.sbuf(128, 8, F32)
        meanb = B.sbuf(128, 512, F32)
        rstdb = B.sbuf(128, 512, F32)
        vtmp = B.sbuf(128, 512, F32)

        afm = B.AF.mark()
        abm = B.AB.mark()
        pA, pB, pC, pD, pE, pF, pG = B.pA, B.pB, B.pC, B.pD, B.pE, B.pF, B.pG
        LD = p.sem(pref + "LD")
        WD = p.sem(pref + "WD")
        QS = p.sem(pref + "QS")
        VS = p.sem(pref + "VS")
        SO = p.sem(pref + "SO")
        MM = p.sem(pref + "MM")
        EV = p.ms["vector"]
        EA = p.ms["scalar"]
        AF_ = B.AF
        AB_ = B.AB
        epscol = ncols[:, 24:25]

        p.dma("sync", cwm[:], wmask_d, LD)
        p.dma("sync", ncols[:], ncols_d, LD)
        p.dma("sync", qkvb[:], qkvb_d, LD)
        p.dma("sync", gateb[:], gateb_d, LD)
        p.dma("sync", ciota[:], iota32_d, LD)
        p.dma("gpsimd", vbrow[:], vbrow_d, WD)
        p.dma("gpsimd", conesb[:], onesb_d, WD)
        sks = AF_.take(16)
        p.dma("sync", sks[0:1, :], sink_d, LD)
        p.wait("scalar", LD, LD.val)
        p.wait("scalar", WD, WD.val)
        ske = p.op("scalar", lambda e: e.activation(sks[0:1, :], sks[0:1, :], AF.Exp), EA)
        p.wait("tensor", EA, ske)
        skm = p.op("tensor", lambda e: e.matmul(pA[:, 0:16], conesf[0:1, :], sks[0:1, :],
                                                start=True, stop=True), inc=MM)
        p.wait("vector", MM, skm)
        p.op("vector", lambda e: e.tensor_copy(sinkb[:], pA[:, 0:16]), EV)
        p.barrier()
        AF_.reset(afm)

        xc = AF_.take(4096)
        xinc = AF_.take(4096)
        sqa = AF_.take(4096)
        qraw = AF_.take(1024)
        ctq = AB_.take(T); stq = AB_.take(T)
        ctk = AB_.take(T); stk = AB_.take(T)
        qkvw = AB_.take(8 * 1536)
        hbf = AB_.take(4096)
        rst = AB_.take(2 * 512)
        vstage = AB_.take(2 * 256)
        p.dma("gpsimd", ctq, ctabq_d, WD)
        p.dma("gpsimd", stq, stabq_d, WD)
        p.dma("gpsimd", ctk, ctabk_d, WD)
        p.dma("gpsimd", stk, stabk_d, WD)
        for c in range(NT):
            cs = slice(512 * c, 512 * (c + 1))
            lv = [p.dma("sync", xc[:, 512 * f:512 * (f + 1)],
                        xT[128 * f:128 * (f + 1), cs], LD) for f in range(NF)]
            wls = [p.dma("gpsimd", qkvw[:, 1536 * kf:1536 * (kf + 1)],
                         qkvw_d[128 * kf:128 * (kf + 1), :], WD) for kf in range(NF)]
            p.wait("scalar", LD, lv[-1])
            p.wait("tensor", LD, lv[-1])
            r_done = ln_chunk(p, xc, sqa, meanb, rstdb, vtmp, conesf, epscol, pA, pB, MM, EV, EA)
            p.wait("vector", EV, r_done)
            for f in range(NF):
                p.op("vector", lambda e, f=f: e.tensor_sub(
                    sqa[:, 512 * f:512 * (f + 1)], xc[:, 512 * f:512 * (f + 1)], meanb[:]))
            a_done = None
            for f in range(NF):
                a_done = p.op("vector", lambda e, f=f: e.tensor_mul(
                    xinc[:, 512 * f:512 * (f + 1)], sqa[:, 512 * f:512 * (f + 1)], rstdb[:]), EV)
            p.wait("sync", EV, a_done)
            for f in range(NF):
                p.dma("sync", xinTd[128 * f:128 * (f + 1), cs],
                      xinc[:, 512 * f:512 * (f + 1)], QS)
            p.wait("scalar", EV, a_done)
            rr = rms_chunk(p, xinc, sqa, rstdb, conesf, epscol, pA, MM, EV, EA)
            p.wait("vector", EV, rr)
            hh = None
            for f in range(NF):
                p.op("vector", lambda e, f=f: e.tensor_mul(
                    sqa[:, 512 * f:512 * (f + 1)], xinc[:, 512 * f:512 * (f + 1)], rstdb[:]))
                hh = p.op("vector", lambda e, f=f: e.tensor_scalar(
                    hbf[:, 512 * f:512 * (f + 1)], sqa[:, 512 * f:512 * (f + 1)],
                    ncols[:, f:f + 1], None, ALU.mult), EV)
            p.wait("tensor", EV, hh)
            p.wait("tensor", WD, WD.val)
            rope_prev = 0
            prev_ev = {0: 0, 1: 0}
            for pr in range(5):
                isq = pr < 4
                for half in (0, 1):
                    et = 2 * pr + half
                    bank = pC if half == 0 else pD
                    if prev_ev[half]:
                        p.wait("tensor", EA, prev_ev[half])
                    mmv = None
                    for kf in range(NF):
                        mmv = p.op("tensor", lambda e, et=et, kf=kf, bank=bank: e.matmul(
                            bank[:], qkvw[:, 1536 * kf + 128 * et:1536 * kf + 128 * (et + 1)],
                            hbf[:, 512 * kf:512 * (kf + 1)],
                            start=(kf == 0), stop=(kf == NF - 1)),
                            inc=(MM if kf == NF - 1 else None))
                    p.wait("scalar", MM, mmv)
                    if pr >= 1:
                        p.wait("scalar", EV, rope_prev)
                    ev = p.op("scalar", lambda e, bank=bank, et=et, half=half: e.activation(
                        qraw[:, 512 * half:512 * (half + 1)], bank[:], AF.Identity,
                        bias=qkvb[:, et:et + 1]), EA)
                    prev_ev[half] = ev
                ct = (ctq if isq else ctk)[:, cs]
                stb = (stq if isq else stk)[:, cs]
                Et = qraw[:, 0:512]
                Ot = qraw[:, 512:1024]
                p.wait("vector", EA, ev)
                p.wait("vector", QS, QS.val)
                p.op("vector", lambda e, Et=Et, ct=ct: e.tensor_mul(sqa[:, 0:512], Et, ct))
                p.op("vector", lambda e, Ot=Ot, stb=stb: e.tensor_mul(sqa[:, 512:1024], Ot, stb))
                p.op("vector", lambda e: e.tensor_sub(rst[:, 0:512], sqa[:, 0:512], sqa[:, 512:1024]))
                p.op("vector", lambda e, Et=Et, stb=stb: e.tensor_mul(sqa[:, 0:512], Et, stb))
                p.op("vector", lambda e, Ot=Ot, ct=ct: e.tensor_mul(sqa[:, 512:1024], Ot, ct))
                rope_prev = p.op("vector", lambda e: e.tensor_add(
                    rst[:, 512:1024], sqa[:, 0:512], sqa[:, 512:1024]), EV)
                p.wait("sync", EV, rope_prev)
                if isq:
                    dstE = qTd[256 * pr:256 * (pr + 1), cs].rearrange("(g r) t -> g r t", r=32)[0::2]
                    dstO = qTd[256 * pr:256 * (pr + 1), cs].rearrange("(g r) t -> g r t", r=32)[1::2]
                else:
                    dstE = kTd[:, cs].rearrange("(g r) t -> g r t", r=32)[0::2]
                    dstO = kTd[:, cs].rearrange("(g r) t -> g r t", r=32)[1::2]
                p.dma("sync", dstE, rst[:, 0:512], QS)
                p.dma("sync", dstO, rst[:, 512:1024], QS)
            vev = {}
            for j in range(4):
                tau = 4 * c + j
                g = j % 2
                bank = pE if g == 0 else pF
                if j >= 2:
                    p.wait("tensor", EV, vev[j - 2])
                for kf in range(NF):
                    p.op("tensor", lambda e, j=j, kf=kf, bank=bank: e.matmul(
                        bank[:, 0:256], hbf[:, 512 * kf + 128 * j:512 * kf + 128 * (j + 1)],
                        qkvw[:, 1536 * kf + 1280:1536 * kf + 1536],
                        start=(kf == 0), stop=False))
                mmv = p.op("tensor", lambda e, bank=bank: e.matmul(
                    bank[:, 0:256], conesb[0:1, :], vbrow[0:1, :],
                    start=False, stop=True), inc=MM)
                p.wait("vector", MM, mmv)
                p.wait("vector", VS, VS.val)
                vst = vstage[:, 256 * g:256 * (g + 1)]
                vev[j] = p.op("vector", lambda e, vst=vst, bank=bank: e.tensor_copy(vst, bank[:, 0:256]), EV)
                p.wait("gpsimd", EV, vev[j])
                p.dma("gpsimd", vtokd[tau, :, :], vst, VS)
            p.barrier()
        AF_.reset(afm); AB_.reset(abm)

        qsl = AB_.take(2 * 512)
        ksl = AB_.take(2 * 128)
        vsl = AB_.take(2 * 66)
        attA = AB_.take(2 * 512)
        attB = AB_.take(2 * 512)
        aosl = AB_.take(4 * 512)
        mtmp = AF_.take(2 * 512)
        rcp = AF_.take(1024)
        rbc = AF_.take(2 * 512)
        p.op("vector", lambda e: e.memset(vsl[:], 1.0), EV)
        QL = p.sem(pref + "QL")
        KDp = [p.sem(pref + "KD0"), p.sem(pref + "KD1")]
        VDp = [p.sem(pref + "VD0"), p.sem(pref + "VD1")]
        QSp = [p.sem(pref + "QS0"), p.sem(pref + "QS1")]
        p.barrier()
        qk_guard = {0: 0, 1: 0}
        slot_guard = {0: 0, 1: 0}
        bank_cons = {0: [], 1: []}
        lastexp_g = 0
        md_prev = 0
        qs_out = {0: 0, 1: 0}
        itn = 0
        for i in range(8):
            g = i // 2
            for c in range(NT):
                cs = slice(512 * c, 512 * (c + 1))
                ipar = itn % 2
                itn += 1
                qslb = qsl[:, 512 * ipar:512 * (ipar + 1)]
                aoslb = aosl[:, 1024 * ipar:1024 * (ipar + 1)]
                p.wait("sync", MM, qk_guard[ipar])
                qv = p.dma("sync", qslb, qTd[128 * i:128 * (i + 1), cs], QL)
                kts = [kt for kt in range(4 * c - 1, 4 * c + 4) if kt >= 0]
                exp_done = {}
                av_done = {}
                kvv = {}
                vvv = {}
                for u, kt in enumerate(kts):
                    par = u % 2
                    kb = ksl[:, 128 * par:128 * (par + 1)]
                    vb = vsl[:, 66 * par:66 * par + 66]
                    p.wait("sync", MM, slot_guard[par])
                    p.dma("sync", kb[0:64, :], kTd[64 * g:64 * (g + 1), 128 * kt:128 * (kt + 1)], KDp[par])
                    kvv[u] = p.dma("sync", kb[64:128, :],
                                   kTd[64 * g:64 * (g + 1), 128 * kt:128 * (kt + 1)], KDp[par])
                    vvv[u] = p.dma("sync", vb[:, 0:64], vtokd[kt, :, 64 * g:64 * (g + 1)], VDp[par])
                    SA = pA if par == 0 else pB
                    SB = pC if par == 0 else pD
                    p.wait("tensor", KDp[par], kvv[u])
                    if u == 0:
                        p.wait("tensor", QL, qv)
                    for (sem2, val2) in bank_cons[par]:
                        p.wait("tensor", sem2, val2)
                    p.op("tensor", lambda e, kb=kb, SA=SA, qslb=qslb: e.matmul(
                        SA[:], kb[0:64, :], qslb[0:64, :], start=True, stop=True))
                    qk = p.op("tensor", lambda e, kb=kb, SB=SB, qslb=qslb: e.matmul(
                        SB[:], kb[64:128, :], qslb[64:128, :], start=True, stop=True,
                        tile_position=(64, 0)), inc=MM)
                    dlt = kt - 4 * c + 1
                    aA = attA[:, 512 * par:512 * (par + 1)]
                    aB = attB[:, 512 * par:512 * (par + 1)]
                    p.wait("vector", MM, qk)
                    p.wait("vector", EA, lastexp_g)
                    p.op("vector", lambda e, SA=SA, dlt=dlt: e.scalar_tensor_tensor(
                        mtmp[:, 0:512], SA[:], 1.0, cwm[:, 512 * dlt:512 * (dlt + 1)],
                        ALU.mult, ALU.add))
                    mk = p.op("vector", lambda e, SB=SB, dlt=dlt: e.scalar_tensor_tensor(
                        mtmp[:, 512:1024], SB[:], 1.0, cwm[:, 512 * dlt:512 * (dlt + 1)],
                        ALU.mult, ALU.add), EV)
                    bank_cons[par] = [(EV, mk)]
                    p.wait("scalar", EV, mk)
                    p.wait("scalar", MM, slot_guard[par])
                    p.op("scalar", lambda e, aA=aA: e.activation(aA, mtmp[:, 0:512], AF.Exp))
                    exp_done[u] = p.op("scalar", lambda e, aB=aB: e.activation(
                        aB, mtmp[:, 512:1024], AF.Exp), EA)
                    lastexp_g = exp_done[u]
                    p.wait("tensor", EA, exp_done[u])
                    p.wait("tensor", VDp[par], vvv[u])
                    if u == 0 and md_prev:
                        p.wait("tensor", EV, md_prev)
                    p.op("tensor", lambda e, vb=vb, aA=aA, u=u, last=len(kts) - 1: e.matmul(
                        pE[0:65, :], vb[:, 0:65], aA, start=(u == 0), stop=(u == last)))
                    av_done[u] = p.op("tensor", lambda e, vb=vb, aB=aB, u=u, last=len(kts) - 1: e.matmul(
                        pF[0:65, :], vb[:, 0:65], aB, start=(u == 0), stop=(u == last)), inc=MM)
                    slot_guard[par] = av_done[u]
                nu = len(kts) - 1
                qk_guard[ipar] = av_done[nu]
                p.wait("vector", MM, av_done[nu])
                p.op("vector", lambda e, i=i: e.tensor_scalar_add(
                    rcp[64:65, 0:512], pE[64:65, :], sinkb[64:65, 2 * i:2 * i + 1]))
                p.op("vector", lambda e: e.reciprocal(rcp[64:65, 0:512], rcp[64:65, 0:512]))
                p.op("vector", lambda e, i=i: e.tensor_scalar_add(
                    rcp[64:65, 512:1024], pF[64:65, :], sinkb[64:65, 2 * i + 1:2 * i + 2]))
                rv = p.op("vector", lambda e: e.reciprocal(rcp[64:65, 512:1024], rcp[64:65, 512:1024]), EV)
                p.wait("tensor", EV, rv)
                p.op("tensor", lambda e: e.matmul(
                    pA[0:64, :], conesf[64:65, 0:64], rcp[64:65, 0:512], start=True, stop=True))
                bv = p.op("tensor", lambda e: e.matmul(
                    pB[0:64, :], conesf[64:65, 0:64], rcp[64:65, 512:1024], start=True, stop=True), inc=MM)
                p.wait("vector", MM, bv)
                p.wait("vector", QSp[ipar], qs_out[ipar])
                p.op("vector", lambda e: e.tensor_copy(rbc[0:64, 0:512], pA[0:64, :]))
                rb2 = p.op("vector", lambda e: e.tensor_copy(rbc[0:64, 512:1024], pB[0:64, :]), EV)
                p.op("vector", lambda e, aoslb=aoslb: e.tensor_mul(
                    aoslb[0:64, 0:512], pE[0:64, :], rbc[0:64, 0:512]))
                md = p.op("vector", lambda e, aoslb=aoslb: e.tensor_mul(
                    aoslb[0:64, 512:1024], pF[0:64, :], rbc[0:64, 512:1024]), EV)
                md_prev = md
                bank_cons[0] = bank_cons[0] + [(EV, rb2)]
                bank_cons[1] = bank_cons[1] + [(EV, rb2)]
                p.wait("scalar", EV, md)
                p.dma("scalar", aoTd[128 * i:128 * i + 64, cs], aoslb[0:64, 0:512], QSp[ipar])
                qs_out[ipar] = p.dma("scalar", aoTd[128 * i + 64:128 * i + 128, cs],
                                     aoslb[0:64, 512:1024], QSp[ipar])
        p.barrier()
        AF_.reset(afm); AB_.reset(abm)

        aoc = AB_.take(4096)
        pw = AB_.take(8 * 1024)
        h2bf = AB_.take(4096)
        xinc2 = AF_.take(4096)
        sq2 = AF_.take(4096)
        h2f = AF_.take(4096)
        glst = AF_.take(512)
        gwt = AF_.take(8 * 32)
        gwls = [p.dma("sync", gwt[:, 32 * kf:32 * (kf + 1)],
                      gatew_d[128 * kf:128 * (kf + 1), :], LD) for kf in range(NF)]
        for c in range(NT):
            cs = slice(512 * c, 512 * (c + 1))
            av = [p.dma("sync", aoc[:, 512 * f:512 * (f + 1)],
                        aoTd[128 * f:128 * (f + 1), cs], LD) for f in range(NF)]
            wls = [p.dma("gpsimd", pw[:, 1024 * kf:1024 * (kf + 1)],
                         projw_d[128 * kf:128 * (kf + 1), :], WD) for kf in range(NF)]
            xv = [p.dma("sync", xinc2[:, 512 * f:512 * (f + 1)],
                        xinTd[128 * f:128 * (f + 1), cs], LD) for f in range(NF)]
            p.wait("tensor", LD, xv[-1])
            p.wait("tensor", WD, wls[-1])
            prj_ea = {}
            for dt_ in range(NF):
                bank = pC if dt_ % 2 == 0 else pD
                if dt_ >= 2:
                    p.wait("tensor", EA, prj_ea[dt_ - 2])
                mmv = None
                for kf in range(NF):
                    mmv = p.op("tensor", lambda e, dt_=dt_, kf=kf, bank=bank: e.matmul(
                        bank[:], pw[:, 1024 * kf + 128 * dt_:1024 * kf + 128 * (dt_ + 1)],
                        aoc[:, 512 * kf:512 * (kf + 1)],
                        start=(kf == 0), stop=(kf == NF - 1)),
                        inc=(MM if kf == NF - 1 else None))
                p.wait("scalar", MM, mmv)
                ev = p.op("scalar", lambda e, bank=bank, dt_=dt_: e.activation(
                    sq2[:, 512 * dt_:512 * (dt_ + 1)], bank[:], AF.Identity,
                    bias=ncols[:, 16 + dt_:16 + dt_ + 1]), EA)
                prj_ea[dt_] = ev
            p.wait("vector", EA, ev)
            ad = None
            for f in range(NF):
                ad = p.op("vector", lambda e, f=f: e.tensor_add(
                    xinc2[:, 512 * f:512 * (f + 1)], xinc2[:, 512 * f:512 * (f + 1)],
                    sq2[:, 512 * f:512 * (f + 1)]), EV)
            p.wait("sync", EV, ad)
            for f in range(NF):
                p.dma("sync", xpTd[128 * f:128 * (f + 1), cs], xinc2[:, 512 * f:512 * (f + 1)], QS)
            p.wait("scalar", EV, ad)
            p.wait("tensor", EV, ad)
            rr = rms_chunk(p, xinc2, sq2, rstdb, conesf, epscol, pA, MM, EV, EA)
            p.wait("vector", EV, rr)
            hh = None
            for f in range(NF):
                p.op("vector", lambda e, f=f: e.tensor_mul(
                    sq2[:, 512 * f:512 * (f + 1)], xinc2[:, 512 * f:512 * (f + 1)], rstdb[:]))
                p.op("vector", lambda e, f=f: e.tensor_scalar(
                    h2f[:, 512 * f:512 * (f + 1)], sq2[:, 512 * f:512 * (f + 1)],
                    ncols[:, 8 + f:8 + f + 1], None, ALU.mult))
                hh = p.op("vector", lambda e, f=f: e.tensor_copy(
                    h2bf[:, 512 * f:512 * (f + 1)], h2f[:, 512 * f:512 * (f + 1)]), EV)
            p.wait("sync", EV, hh)
            for f in range(NF):
                p.dma("sync", h2Td[128 * f:128 * (f + 1), cs], h2bf[:, 512 * f:512 * (f + 1)], QS)
            p.wait("tensor", EV, hh)
            mmv = None
            for kf in range(NF):
                mmv = p.op("tensor", lambda e, kf=kf: e.matmul(
                    pE[0:32, :], gwt[:, 32 * kf:32 * (kf + 1)],
                    h2f[:, 512 * kf:512 * (kf + 1)],
                    start=(kf == 0), stop=(kf == NF - 1)), inc=(MM if kf == NF - 1 else None))
            p.wait("scalar", MM, mmv)
            ge = p.op("scalar", lambda e: e.activation(glst[0:32, :], pE[0:32, :], AF.Identity,
                                                       bias=gateb[:, 0:1]), EA)
            p.wait("tensor", EA, ge)
            gcp = 0
            for j in range(4):
                if gcp:
                    p.wait("tensor", EV, gcp)
                tm = p.op("tensor", lambda e, j=j: e.transpose(
                    pF[:, 0:32], glst[0:32, 128 * j:128 * (j + 1)], B.cidf[0:32, 0:32]), inc=MM)
                p.wait("vector", MM, tm)
                gcp = p.op("vector", lambda e, c=c, j=j: e.tensor_copy(
                    gltok[:, 32 * (4 * c + j):32 * (4 * c + j + 1)], pF[:, 0:32]), EV)
            p.barrier()
        AF_.reset(afm); AB_.reset(abm)

        for b in range(NB):
            gl = gltok[:, 32 * b:32 * (b + 1)]
            cb = comb[:, 32 * b:32 * (b + 1)]
            p.op("vector", lambda e, gl=gl: e.max(m8[:], gl))
            p.op("vector", lambda e, gl=gl: e.max_index(idxu[:], m8[:], gl))
            p.op("vector", lambda e: e.tensor_copy(idxf[:], idxu[:]))
            p.op("vector", lambda e: e.tensor_scalar_mul(vtmp[:, 0:1], m8[:, 0:1], -1.0))
            ee = p.op("vector", lambda e: e.engine_nop(), EV)
            p.wait("scalar", EV, ee)
            xx = p.op("scalar", lambda e: e.activation(w4[:, 0:4], m8[:, 0:4], AF.Exp,
                                                       bias=vtmp[:, 0:1]), EA)
            p.wait("vector", EA, xx)
            p.op("vector", lambda e: e.reduce_sum(vtmp[:, 1:2], w4[:, 0:4], AX))
            p.op("vector", lambda e: e.reciprocal(vtmp[:, 2:3], vtmp[:, 1:2]))
            p.op("vector", lambda e: e.tensor_scalar_mul(w4[:, 0:4], w4[:, 0:4], vtmp[:, 2:3]))
            p.op("vector", lambda e, cb=cb: e.memset(cb, 0.0))
            for kk in range(4):
                p.op("vector", lambda e, kk=kk: e.tensor_scalar(
                    vtmp[:, 3:35], ciota[:], idxf[:, kk:kk + 1], w4[:, kk:kk + 1],
                    ALU.is_equal, ALU.mult))
                p.op("vector", lambda e, cb=cb: e.tensor_add(cb, cb, vtmp[:, 3:35]))
        p.barrier()

        wup_sb = [AB_.take(8 * 2048), AB_.take(8 * 2048)]
        wdn_sb = AB_.take(8 * 1024)
        h2q = AB_.take(NF * QT)
        silu_sb = AB_.take(2 * 512)
        vv_sb = AB_.take(8 * 512)
        y2q = AF_.take(QF * 1024)
        WU = [p.sem(pref + "WU0"), p.sem(pref + "WU1")]
        WN = p.sem(pref + "WN0")
        H2L = p.sem(pref + "H2L")
        mm_prev_w = {0: 0, 1: 0}
        mm_prev_all = 0
        vmul = {}
        vv_guard = 0
        zgate = {0: 0, 1: 0}
        for q in range(NQ):
            h2v = [p.dma("sync", h2q[:, QT * f:QT * (f + 1)],
                         h2Td[128 * f:128 * (f + 1), QT * q:QT * (q + 1)], H2L)
                   for f in range(NF)]
            p.op("vector", lambda e: e.memset(y2q[:], 0.0), EV)
            for ei in range(E):
                par = ei % 2
                wub = wup_sb[par]
                wdb = wdn_sb
                p.wait("gpsimd", MM, mm_prev_w[par])
                wuv = [p.dma("gpsimd", wub[:, 2048 * kf:2048 * (kf + 1)],
                             wup_d[ei, 128 * kf:128 * (kf + 1), :], WU[par]) for kf in range(NF)]
                p.wait("gpsimd", MM, mm_prev_all)
                wnv = [p.dma("gpsimd", wdb[:, 1024 * kf:1024 * (kf + 1)],
                             wdn_d[ei, 128 * kf:128 * (kf + 1), :], WN) for kf in range(NF)]
                p.wait("tensor", WU[par], wuv[-1])
                p.wait("tensor", H2L, h2v[-1])
                last_mm_e = 0
                for tc in range(NTC):
                    for dt_ in range(8):
                        bku = pA if dt_ % 2 == 0 else pB
                        bkg = pC if dt_ % 2 == 0 else pD
                        if vmul.get((dt_ - 2) % 8):
                            p.wait("tensor", EV, vmul[(dt_ - 2) % 8])
                        if dt_ < 2 and vv_guard:
                            p.wait("vector", MM, vv_guard)
                        mu = mg = None
                        for kf in range(NF):
                            mu = p.op("tensor", lambda e, kf=kf, dt_=dt_, bku=bku, tc=tc, wub=wub: e.matmul(
                                bku[:, 0:TCW], wub[:, 2048 * kf + 128 * dt_:2048 * kf + 128 * (dt_ + 1)],
                                h2q[:, QT * kf + TCW * tc:QT * kf + TCW * (tc + 1)],
                                start=(kf == 0), stop=(kf == NF - 1)))
                        for kf in range(NF):
                            mg = p.op("tensor", lambda e, kf=kf, dt_=dt_, bkg=bkg, tc=tc, wub=wub: e.matmul(
                                bkg[:, 0:TCW], wub[:, 2048 * kf + 128 * (dt_ + 8):2048 * kf + 128 * (dt_ + 9)],
                                h2q[:, QT * kf + TCW * tc:QT * kf + TCW * (tc + 1)],
                                start=(kf == 0), stop=(kf == NF - 1)),
                                inc=(MM if kf == NF - 1 else None))
                        p.wait("scalar", MM, mg)
                        if vmul.get((dt_ - 2) % 8):
                            p.wait("scalar", EV, vmul[(dt_ - 2) % 8])
                        slb = silu_sb[:, 512 * (dt_ % 2):512 * (dt_ % 2) + TCW]
                        sl = p.op("scalar", lambda e, bkg=bkg, slb=slb: e.activation(slb, bkg[:, 0:TCW], AF.Silu), EA)
                        p.wait("vector", EA, sl)
                        vmul[dt_] = p.op("vector", lambda e, dt_=dt_, bku=bku, slb=slb: e.scalar_tensor_tensor(
                            vv_sb[:, TCW * dt_:TCW * (dt_ + 1)], bku[:, 0:TCW], 1.0, slb,
                            ALU.mult, ALU.mult), EV)
                    p.wait("tensor", EV, vmul[7])
                    p.wait("tensor", WN, wnv[-1])
                    for tt in range(TTS):
                        tglob = TTS * tc + tt
                        for fc in range(2):
                            bank = pE if fc == 0 else pF
                            if zgate[fc]:
                                p.wait("tensor", EV, zgate[fc])
                            mz = None
                            for kf in range(NF):
                                mz = p.op("tensor", lambda e, kf=kf, tt=tt, fc=fc, bank=bank, wdb=wdb: e.matmul(
                                    bank[:], vv_sb[:, TCW * kf + 128 * tt:TCW * kf + 128 * (tt + 1)],
                                    wdb[:, 1024 * kf + 512 * fc:1024 * kf + 512 * (fc + 1)],
                                    start=(kf == 0), stop=(kf == NF - 1)),
                                    inc=(MM if kf == NF - 1 else None))
                            last_mm_e = mz
                            p.wait("vector", MM, mz)
                            cbl = comb[:, 32 * (QF * q + tglob) + ei:32 * (QF * q + tglob) + ei + 1]
                            zgate[fc] = p.op("vector", lambda e, bank=bank, tglob=tglob, fc=fc, cbl=cbl: e.scalar_tensor_tensor(
                                y2q[:, 1024 * tglob + 512 * fc:1024 * tglob + 512 * (fc + 1)],
                                bank[:], cbl, y2q[:, 1024 * tglob + 512 * fc:1024 * tglob + 512 * (fc + 1)],
                                ALU.mult, ALU.add), EV)
                    vv_guard = last_mm_e
                mm_prev_w[par] = last_mm_e
                mm_prev_all = last_mm_e
            p.wait("tensor", EV, EV.val)
            ycp = 0
            for tt in range(QF):
                for f in range(NF):
                    if ycp:
                        p.wait("tensor", EV, ycp)
                    tm = p.op("tensor", lambda e, tt=tt, f=f: e.transpose(
                        pG[:, 0:128], y2q[:, 1024 * tt + 128 * f:1024 * tt + 128 * (f + 1)],
                        B.cidf[:]), inc=MM)
                    p.wait("vector", MM, tm)
                    p.wait("vector", SO, SO.val)
                    ycp = p.op("vector", lambda e: e.tensor_copy(vtmp[:, 0:128], pG[:, 0:128]), EV)
                    p.wait("sync", EV, ycp)
                    p.dma("sync", y2Td[128 * f:128 * (f + 1),
                                       QT * q + 128 * tt:QT * q + 128 * (tt + 1)], vtmp[:, 0:128], SO)
            p.barrier()
        AF_.reset(afm); AB_.reset(abm)

        xpc = AF_.take(4096)
        y2c = AF_.take(4096)
        sq3 = y2c
        x1c = AF_.take(2048)
        X1L = p.sem(pref + "X1L")
        for c in range(NT):
            cs = slice(512 * c, 512 * (c + 1))
            pv = [p.dma("sync", xpc[:, 512 * f:512 * (f + 1)],
                        xpTd[128 * f:128 * (f + 1), cs], LD) for f in range(NF)]
            yv = [p.dma("sync", y2c[:, 512 * f:512 * (f + 1)],
                        y2Td[128 * f:128 * (f + 1), cs], LD) for f in range(NF)]
            p.wait("vector", LD, yv[-1])
            sd = None
            for f in range(NF):
                sd = p.op("vector", lambda e, f=f: e.tensor_add(
                    xpc[:, 512 * f:512 * (f + 1)], xpc[:, 512 * f:512 * (f + 1)],
                    y2c[:, 512 * f:512 * (f + 1)]), EV)
            p.wait("scalar", EV, sd)
            p.wait("tensor", EV, sd)
            r_done = ln_chunk(p, xpc, sq3, meanb, rstdb, vtmp, conesf, epscol, pA, pB, MM, EV, EA)
            p.wait("vector", EV, r_done)
            for hf in range(2):
                p.wait("sync", SO, SO.val)
                x1v = [p.dma("sync", x1c[:, 512 * f:512 * (f + 1)],
                             xT[128 * (4 * hf + f):128 * (4 * hf + f) + 128, cs], X1L)
                       for f in range(4)]
                p.wait("vector", X1L, x1v[-1])
                od = None
                for f in range(4):
                    ff = 4 * hf + f
                    p.op("vector", lambda e, ff=ff, f=f: e.tensor_sub(
                        sq3[:, 512 * ff:512 * (ff + 1)], xpc[:, 512 * ff:512 * (ff + 1)], meanb[:]))
                    p.op("vector", lambda e, ff=ff: e.tensor_mul(
                        sq3[:, 512 * ff:512 * (ff + 1)], sq3[:, 512 * ff:512 * (ff + 1)], rstdb[:]))
                    od = p.op("vector", lambda e, ff=ff, f=f: e.tensor_add(
                        x1c[:, 512 * f:512 * (f + 1)], x1c[:, 512 * f:512 * (f + 1)],
                        sq3[:, 512 * ff:512 * (ff + 1)]), EV)
                p.wait("sync", EV, od)
                for f in range(4):
                    p.dma("sync", x2T[128 * (4 * hf + f):128 * (4 * hf + f) + 128, cs],
                          x1c[:, 512 * f:512 * (f + 1)], SO)
            p.barrier()
        p.wait("sync", SO, SO.val)
        p.barrier()
        B.AF.reset(afm0)
        B.AB.reset(abm0)




def build_l0_ffn(T=4096):
    B = Bundle(T=T)
    xT = B.nc.dram_tensor("xT", [D, T], F32, kind="ExternalInput").ap()
    x2T = B.nc.dram_tensor("x2T", [D, T], F32, kind="ExternalOutput").ap()
    emit_l0_ffn(B, T, "", xT, x2T)
    return B.finish()


def emit_l1_attn(B, T, pref, xT, x3T):
    nc = B.nc
    p = B.p
    conesf = B.conesf
    epsc = B.epsc
    afm0 = B.AF.mark()
    abm0 = B.AB.mark()
    NT = T // 512
    DI = 2048
    NDF = DI // 128
    NG = DI // 8
    GPF = 16
    NH = 2
    HT = T // NH
    SCW = min(512, HT)
    NCH = HT // SCW

    inw_d = nc.dram_tensor(pref + "inw", [D, 2 * DI], F32, kind="ExternalInput").ap()
    xpw_d = nc.dram_tensor(pref + "xpw", [DI, 160], F32, kind="ExternalInput").ap()
    dtw_d = nc.dram_tensor(pref + "dtw", [128, DI], F32, kind="ExternalInput").ap()
    outw_d = nc.dram_tensor(pref + "outw", [DI, D], F32, kind="ExternalInput").ap()
    convw_d = nc.dram_tensor(pref + "convw", [128, 4 * NDF], F32, kind="ExternalInput").ap()
    dtb_d = nc.dram_tensor(pref + "dtb", [128, NDF], F32, kind="ExternalInput").ap()
    convb_d = nc.dram_tensor(pref + "convb", [128, NDF], F32, kind="ExternalInput").ap()
    dcol_d = nc.dram_tensor(pref + "dcol", [128, NDF], F32, kind="ExternalInput").ap()
    alogc_d = nc.dram_tensor(pref + "alogc", [128, NG], F32, kind="ExternalInput").ap()
    selbc_d = nc.dram_tensor(pref + "selbc", [32, 256], F32, kind="ExternalInput").ap()
    rep8_d = nc.dram_tensor(pref + "rep8", [8, 128], F32, kind="ExternalInput").ap()
    rep8b_d = nc.dram_tensor(pref + "rep8b", [8, 128], BF16, kind="ExternalInput").ap()
    rsel_d = nc.dram_tensor(pref + "rsel", [128, 16 * 128], BF16, kind="ExternalInput").ap()
    xbrawd = nc.dram_tensor(pref + "xbrawd", [DI, T], F32, kind="Internal").ap()
    zbsd = nc.dram_tensor(pref + "zbsd", [DI, T], BF16, kind="Internal").ap()
    xbcd = nc.dram_tensor(pref + "xbcd", [DI, T], BF16, kind="Internal").ap()
    dtd = nc.dram_tensor(pref + "dtd", [DI, T], F32, kind="Internal").ap()
    ud = nc.dram_tensor(pref + "ud", [DI, T], BF16, kind="Internal").ap()
    ybzd = nc.dram_tensor(pref + "ybzd", [DI, T], BF16, kind="Internal").ap()

    if True:
        cselbc = B.sbuf(32, 256, F32)
        crep8 = B.sbuf(8, 128, F32)
        cconvw = B.sbuf(128, 4 * NDF, F32)
        cdtb = B.sbuf(128, NDF, F32)
        cconvb = B.sbuf(128, NDF, F32)
        cdcol = B.sbuf(128, NDF, F32)
        acols = B.sbuf(128, NG, F32)
        hstate = B.sbuf(128, GPF, F32)
        crep8b = B.sbuf(8, 128, BF16)
        crsel = B.sbuf(128, 16 * 128, BF16)
        Btile = B.sbuf(128, T, BF16)
        Ctile = B.sbuf(128, T, BF16)
        afm = B.AF.mark()
        abm = B.AB.mark()
        BCall = B.sbuf(32, T, F32)
        afmBC = B.AF.mark()

        pA, pB, pC, pD, pG = B.pA, B.pB, B.pC, B.pD, B.pG
        pY = [B.pC, B.pD, B.pE, B.pF]
        LD = p.sem(pref + "LD")
        WD = p.sem(pref + "WD")
        QS = p.sem(pref + "QS")
        US = p.sem(pref + "US")
        SO = p.sem(pref + "SO")
        MM = p.sem(pref + "MM")
        EV = p.ms["vector"]
        EA = p.ms["scalar"]
        GP = p.ms["gpsimd"]
        AF_ = B.AF
        AB_ = B.AB

        p.dma("sync", cselbc[:], selbc_d, LD)
        p.dma("sync", crep8[:], rep8_d, LD)
        p.dma("sync", crep8b[:], rep8b_d, LD)
        p.dma("sync", crsel[:], rsel_d, LD)
        p.dma("sync", cconvw[:], convw_d, LD)
        p.dma("sync", cdtb[:], dtb_d, LD)
        p.dma("sync", cconvb[:], convb_d, LD)
        p.dma("sync", cdcol[:], dcol_d, LD)
        p.dma("sync", acols[:], alogc_d, LD)
        p.wait("scalar", LD, LD.val)
        ae = p.op("scalar", lambda e: e.activation(acols[:], acols[:], AF.Exp), EA)
        p.wait("vector", EA, ae)
        p.op("vector", lambda e: e.tensor_scalar_mul(acols[:], acols[:], -1.0), EV)
        p.barrier()

        xc = AF_.take(4096)
        sqa = AF_.take(4096)
        est = AF_.take(1024)
        meanb = AF_.take(512)
        rstdb = AF_.take(512)
        vtmp = AF_.take(512)
        xnb = AB_.take(4096)
        inw = AB_.take(8 * 2048)
        ebt = AB_.take(1024)
        for c in range(NT):
            cs = slice(512 * c, 512 * (c + 1))
            lv = [p.dma("sync", xc[:, 512 * f:512 * (f + 1)],
                        xT[128 * f:128 * (f + 1), cs], LD) for f in range(NF)]
            p.wait("scalar", LD, lv[-1])
            p.wait("tensor", LD, lv[-1])
            r_done = ln_chunk(p, xc, sqa, meanb, rstdb, vtmp, conesf, epsc[:, 0:1], pA, pB, MM, EV, EA)
            p.wait("vector", EV, r_done)
            for f in range(NF):
                p.op("vector", lambda e, f=f: e.tensor_sub(
                    sqa[:, 512 * f:512 * (f + 1)], xc[:, 512 * f:512 * (f + 1)], meanb[:]))
            a_done = None
            for f in range(NF):
                a_done = p.op("vector", lambda e, f=f: e.tensor_mul(
                    xnb[:, 512 * f:512 * (f + 1)], sqa[:, 512 * f:512 * (f + 1)], rstdb[:]), EV)
            p.wait("tensor", EV, a_done)
            for wh in range(2):
                if wh == 1:
                    p.wait("gpsimd", MM, MM.val)
                    p.wait("tensor", EV, EV.val)
                wls = [p.dma("gpsimd", inw[:, 2048 * kf:2048 * (kf + 1)],
                             inw_d[128 * kf:128 * (kf + 1), 2048 * wh:2048 * (wh + 1)], WD)
                       for kf in range(NF)]
                p.wait("tensor", WD, wls[-1])
                ev_prev = {0: 0, 1: 0}
                for et in range(NDF):
                    par = et % 2
                    bank = pC if par == 0 else pD
                    if ev_prev[par]:
                        p.wait("tensor", EV if wh == 0 else EA, ev_prev[par])
                    mmv = None
                    for kf in range(NF):
                        mmv = p.op("tensor", lambda e, et=et, kf=kf, bank=bank: e.matmul(
                            bank[:], inw[:, 2048 * kf + 128 * et:2048 * kf + 128 * (et + 1)],
                            xnb[:, 512 * kf:512 * (kf + 1)],
                            start=(kf == 0), stop=(kf == NF - 1)),
                            inc=(MM if kf == NF - 1 else None))
                    if wh == 0:
                        p.wait("vector", MM, mmv)
                        p.wait("vector", QS, QS.val)
                        eb = est[:, 512 * par:512 * (par + 1)]
                        ev = p.op("vector", lambda e, bank=bank, eb=eb: e.tensor_copy(eb, bank[:]), EV)
                        ev_prev[par] = ev
                        p.wait("sync", EV, ev)
                        p.dma("sync", xbrawd[128 * et:128 * (et + 1), cs], eb, QS)
                    else:
                        p.wait("scalar", MM, mmv)
                        p.wait("scalar", QS, QS.val)
                        eb = ebt[:, 512 * par:512 * (par + 1)]
                        ev = p.op("scalar", lambda e, bank=bank, eb=eb: e.activation(eb, bank[:], AF.Silu), EA)
                        ev_prev[par] = ev
                        p.wait("scalar", EA, ev)
                        p.dma("scalar", zbsd[128 * et:128 * (et + 1), cs], eb, QS)
            p.barrier()
        AF_.reset(afmBC); AB_.reset(abm)

        xbt = AF_.take(4 + T)
        cvt = AF_.take(T)
        xcb = AB_.take(T)
        for ft in range(NDF):
            p.wait("vector", QS, QS.val)
            z = p.op("vector", lambda e: e.memset(xbt[:, 0:4], 0.0), EV)
            p.wait("sync", EV, z)
            lv = p.dma("sync", xbt[:, 4:4 + T], xbrawd[128 * ft:128 * (ft + 1), :], LD)
            p.wait("vector", LD, lv)
            p.op("vector", lambda e, ft=ft: e.tensor_scalar_mul(
                cvt[:], xbt[:, 1:1 + T], cconvw[:, 4 * ft:4 * ft + 1]))
            cd = None
            for kk in range(1, 4):
                cd = p.op("vector", lambda e, ft=ft, kk=kk: e.scalar_tensor_tensor(
                    cvt[:], xbt[:, 1 + kk:1 + kk + T], cconvw[:, 4 * ft + kk:4 * ft + kk + 1],
                    cvt[:], ALU.mult, ALU.add), EV)
            p.wait("scalar", EV, cd)
            p.wait("scalar", QS, QS.val)
            sd = p.op("scalar", lambda e, ft=ft: e.activation(
                xcb[:], cvt[:], AF.Silu, bias=cconvb[:, ft:ft + 1]), EA)
            p.wait("scalar", EA, sd)
            p.dma("scalar", xbcd[128 * ft:128 * (ft + 1), :], xcb[:], QS)
        p.barrier()
        AF_.reset(afmBC); AB_.reset(abm)

        dtr = AF_.take(512)
        spt = AF_.take(2 * 512)
        dtt = AF_.take(512)
        dtwb = AF_.take(2048)
        xcbc = AB_.take(NDF * 512)
        xpwb = AB_.take(NDF * 160)
        ubt = AB_.take(512)
        xpls = [p.dma("gpsimd", xpwb[:, 160 * kf:160 * (kf + 1)],
                      xpw_d[128 * kf:128 * (kf + 1), :], WD) for kf in range(NDF)]
        dtls = p.dma("sync", dtwb[:], dtw_d, LD)
        for c in range(NT):
            cs = slice(512 * c, 512 * (c + 1))
            lv = [p.dma("sync", xcbc[:, 512 * f:512 * (f + 1)],
                        xbcd[128 * f:128 * (f + 1), cs], LD) for f in range(NDF)]
            p.wait("tensor", LD, lv[-1])
            p.wait("tensor", WD, WD.val)
            mm2 = None
            for kf in range(NDF):
                p.op("tensor", lambda e, kf=kf: e.matmul(
                    pA[:], xpwb[:, 160 * kf:160 * kf + 128], xcbc[:, 512 * kf:512 * (kf + 1)],
                    start=(kf == 0), stop=(kf == NDF - 1)))
            for kf in range(NDF):
                mm2 = p.op("tensor", lambda e, kf=kf: e.matmul(
                    pB[0:32, :], xpwb[:, 160 * kf + 128:160 * kf + 160], xcbc[:, 512 * kf:512 * (kf + 1)],
                    start=(kf == 0), stop=(kf == NDF - 1)), inc=(MM if kf == NDF - 1 else None))
            p.wait("vector", MM, mm2)
            dte = p.op("vector", lambda e: e.tensor_copy(dtr[:], pA[:]), EV)
            bce = p.op("vector", lambda e, cs=cs: e.tensor_copy(BCall[:, cs], pB[0:32, :]), EV)
            p.wait("tensor", EV, dte)
            dt_ev = {0: 0, 1: 0}
            prev_uu = 0
            for et in range(NDF):
                par = et % 2
                bank = pC if par == 0 else pD
                if dt_ev[par]:
                    p.wait("tensor", EV, dt_ev[par])
                mmv = p.op("tensor", lambda e, et=et, bank=bank: e.matmul(
                    bank[:], dtwb[:, 128 * et:128 * (et + 1)], dtr[:],
                    start=True, stop=True), inc=MM)
                p.wait("scalar", MM, mmv)
                t0 = spt[:, 0:512]
                t1 = spt[:, 512:1024]
                p.wait("scalar", EV, prev_uu)
                p.op("scalar", lambda e, bank=bank, et=et, t0=t0: e.activation(
                    t0, bank[:], AF.Identity, bias=cdtb[:, et:et + 1]), EA)
                p.op("scalar", lambda e, t0=t0, t1=t1: e.activation(t1, t0, AF.Abs), EA)
                p.op("scalar", lambda e, t1=t1: e.activation(t1, t1, AF.Exp, scale=-1.0), EA)
                sp2 = p.op("scalar", lambda e, t1=t1: e.activation(
                    t1, t1, AF.Ln, bias=conesf[:, 0:1]), EA)
                p.wait("vector", EA, sp2)
                p.wait("vector", QS, QS.val)
                p.op("vector", lambda e, t0=t0: e.tensor_scalar_max(dtt[:], t0, 0.0))
                p.op("vector", lambda e, t1=t1: e.tensor_add(dtt[:], dtt[:], t1))
                uu = p.op("vector", lambda e, et=et: e.tensor_mul(
                    ubt[:], dtt[:], xcbc[:, 512 * et:512 * (et + 1)]), EV)
                dt_ev[par] = uu
                prev_uu = uu
                p.wait("sync", EV, uu)
                p.dma("sync", dtd[128 * et:128 * (et + 1), cs], dtt[:], QS)
                p.dma("sync", ud[128 * et:128 * (et + 1), cs], ubt[:], QS)
            p.barrier()
        AF_.reset(afmBC); AB_.reset(abm)

        bce_ev = 0
        for c in range(NT):
            cs = slice(512 * c, 512 * (c + 1))
            if bce_ev:
                p.wait("tensor", EV, bce_ev)
            p.op("tensor", lambda e, cs=cs: e.matmul(
                pC[:], cselbc[:, 0:128], BCall[:, cs], start=True, stop=True))
            mmC = p.op("tensor", lambda e, cs=cs: e.matmul(
                pD[:], cselbc[:, 128:256], BCall[:, cs], start=True, stop=True), inc=MM)
            p.wait("vector", MM, mmC)
            p.op("vector", lambda e, cs=cs: e.tensor_copy(Btile[:, cs], pC[:]))
            bce_ev = p.op("vector", lambda e, cs=cs: e.tensor_copy(Ctile[:, cs], pD[:]), EV)
        p.barrier()

        AF_.reset(afm)
        dts = AF_.take(2 * HT)
        decay = AF_.take(2 * HT)
        inpb = AF_.take(2 * HT)
        hbuf = AF_.take(2 * HT)
        ytmp = AF_.take(512)
        uts = AB_.take(2 * HT)
        hcb = AB_.take(2 * HT)
        ybz = AB_.take(2 * HT)
        xbch = AB_.take(2 * HT)
        zbh = AB_.take(2 * HT)
        DTL = [p.sem(pref + "DTL0"), p.sem(pref + "DTL1")]
        UL = [p.sem(pref + "UL0"), p.sem(pref + "UL1")]
        XBLp = [p.sem(pref + "XBL0"), p.sem(pref + "XBL1")]
        USp = [p.sem(pref + "US0"), p.sem(pref + "US1")]
        lastrep = {0: 0, 1: 0}
        lastread = {0: 0, 1: 0}
        lastscan = {0: 0, 1: 0}
        lasthc = {0: 0, 1: 0}
        lastexp = {}
        lastexpg = {0: 0, 1: 0}
        lastinp2 = {0: 0, 1: 0}
        lastyev = {0: 0, 1: 0}
        usv = {0: 0, 1: 0}
        xzlv = {0: 0, 1: 0}
        lastpy0 = [0]
        for ft in range(NDF):
            for half in range(NH):
                hs = slice(HT * half, HT * (half + 1))
                hpar = (NH * ft + half) % 2
                xbch_ = xbch[:, HT * hpar:HT * (hpar + 1)]
                zbh_ = zbh[:, HT * hpar:HT * (hpar + 1)]
                ybz_ = ybz[:, HT * hpar:HT * (hpar + 1)]
                p.wait("sync", EV, lastyev[hpar])
                p.dma("sync", xbch_, xbcd[128 * ft:128 * (ft + 1), hs], XBLp[hpar])
                xzlv[hpar] = p.dma("sync", zbh_, zbsd[128 * ft:128 * (ft + 1), hs], XBLp[hpar])
                def d_front(gi, half=half, hs=hs):
                    g = GPF * ft + gi
                    gpar = gi % 2
                    dcy = decay[:, HT * gpar:HT * (gpar + 1)]
                    inp_ = inpb[:, HT * gpar:HT * (gpar + 1)]
                    p.wait("sync", MM, lastrep[gpar])
                    dv = p.dma("sync", dts[0:8, HT * gpar:HT * (gpar + 1)],
                               dtd[8 * g:8 * (g + 1), hs], DTL[gpar])
                    uv = p.dma("sync", uts[0:8, HT * gpar:HT * (gpar + 1)],
                               ud[8 * g:8 * (g + 1), hs], UL[gpar])
                    repv = None
                    for ch in range(NCH):
                        cpar = ch % 2
                        dbank = pA
                        ubank = pB if cpar == 0 else pG
                        p.wait("tensor", DTL[gpar], dv)
                        p.wait("tensor", UL[gpar], uv)
                        if lastexp.get(0):
                            p.wait("tensor", EA, lastexp[0])
                        if lastinp2.get(cpar):
                            p.wait("tensor", EV, lastinp2[cpar])
                        p.op("tensor", lambda e, ch=ch, gpar=gpar, dbank=dbank: e.matmul(
                            dbank[:, 0:SCW], crep8[:], dts[0:8, HT * gpar + SCW * ch:HT * gpar + SCW * (ch + 1)],
                            start=True, stop=True))
                        repv = p.op("tensor", lambda e, ch=ch, gpar=gpar, ubank=ubank: e.matmul(
                            ubank[:, 0:SCW], crep8b[:], uts[0:8, HT * gpar + SCW * ch:HT * gpar + SCW * (ch + 1)],
                            start=True, stop=True), inc=MM)
                        p.wait("scalar", MM, repv)
                        if lastscan[gpar]:
                            p.wait("scalar", EV, lastscan[gpar])
                        lastexp[0] = p.op("scalar", lambda e, ch=ch, g=g, dbank=dbank, dcy=dcy: e.activation(
                            dcy[:, SCW * ch:SCW * (ch + 1)], dbank[:, 0:SCW], AF.Exp,
                            scale=acols[:, g:g + 1]), EA)
                        p.wait("vector", MM, repv)
                        lastinp2[cpar] = p.op("vector", lambda e, ch=ch, half=half, ubank=ubank, inp_=inp_: e.tensor_mul(
                            inp_[:, SCW * ch:SCW * (ch + 1)], ubank[:, 0:SCW],
                            Btile[:, HT * half + SCW * ch:HT * half + SCW * (ch + 1)]), EV)
                    lastrep[gpar] = repv
                    lastexpg[gpar] = lastexp[0]

                def d_scanhc(gi, half=half, hs=hs):
                    gpar = gi % 2
                    dcy = decay[:, HT * gpar:HT * (gpar + 1)]
                    inp_ = inpb[:, HT * gpar:HT * (gpar + 1)]
                    hbf_ = hbuf[:, HT * gpar:HT * (gpar + 1)]
                    hcb_ = hcb[:, HT * gpar:HT * (gpar + 1)]
                    p.wait("vector", EA, lastexpg[gpar])
                    if lasthc[gpar]:
                        p.wait("vector", GP, lasthc[gpar])
                    if half == 0:
                        sc = p.op("vector", lambda e, hbf_=hbf_, dcy=dcy, inp_=inp_: e.tensor_tensor_scan(
                            hbf_, dcy, inp_, 0.0, ALU.mult, ALU.add), EV)
                    else:
                        sc = p.op("vector", lambda e, gi=gi, hbf_=hbf_, dcy=dcy, inp_=inp_: e.tensor_tensor_scan(
                            hbf_, dcy, inp_, hstate[:, gi:gi + 1], ALU.mult, ALU.add), EV)
                    lastscan[gpar] = sc
                    if half == 0:
                        sc = p.op("vector", lambda e, gi=gi, hbf_=hbf_: e.tensor_copy(
                            hstate[:, gi:gi + 1], hbf_[:, HT - 1:HT]), EV)
                    p.wait("gpsimd", EV, sc)
                    p.wait("gpsimd", MM, lastread[gpar])
                    lasthc[gpar] = p.op("gpsimd", lambda e, half=half, hcb_=hcb_, hbf_=hbf_, hs=hs: e.tensor_tensor(
                        hcb_, hbf_, Ctile[:, hs], ALU.mult), GP)

                def d_py(gi):
                    gpar = gi % 2
                    hcb_ = hcb[:, HT * gpar:HT * (gpar + 1)]
                    p.wait("tensor", GP, lasthc[gpar])
                    if gi == 0 and lastpy0[0]:
                        p.wait("tensor", EV, lastpy0[0])
                    for j in range(NCH):
                        lastread[gpar] = p.op("tensor", lambda e, j=j, gi=gi, hcb_=hcb_: e.matmul(
                            pY[j][:, 0:SCW], crsel[:, 128 * gi:128 * (gi + 1)],
                            hcb_[:, SCW * j:SCW * (j + 1)],
                            start=(gi == 0), stop=(gi == GPF - 1)), inc=MM)

                for gi in range(GPF):
                    d_front(gi)
                    if gi >= 1:
                        d_scanhc(gi - 1)
                    if gi >= 2:
                        d_py(gi - 2)
                d_scanhc(GPF - 1)
                d_py(GPF - 2)
                d_py(GPF - 1)
                p.wait("vector", MM, lastread[(GPF - 1) % 2])
                p.wait("vector", XBLp[hpar], xzlv[hpar])
                p.wait("vector", USp[hpar], usv[hpar])
                yev = None
                for j in range(NCH):
                    p.op("vector", lambda e, j=j, ft=ft, xbch_=xbch_: e.scalar_tensor_tensor(
                        ytmp[:, 0:SCW], xbch_[:, SCW * j:SCW * (j + 1)], cdcol[:, ft:ft + 1],
                        pY[j][:, 0:SCW], ALU.mult, ALU.add))
                    yev = p.op("vector", lambda e, j=j, ybz_=ybz_, zbh_=zbh_: e.tensor_mul(
                        ybz_[:, SCW * j:SCW * (j + 1)], ytmp[:, 0:SCW], zbh_[:, SCW * j:SCW * (j + 1)]), EV)
                p.wait("sync", EV, yev)
                usv[hpar] = p.dma("sync", ybzd[128 * ft:128 * (ft + 1), hs], ybz_, USp[hpar])
                lastyev[hpar] = yev
                lastpy0[0] = yev
        p.barrier()
        AF_.reset(afm); AB_.reset(abm)

        yc = AB_.take(NDF * 512)
        oww = AB_.take(NDF * 1024)
        boc = AF_.take(4096)
        sq3 = AF_.take(4096)
        x2c = AF_.take(4096)
        meanb2 = AF_.take(512)
        rstdb2 = AF_.take(512)
        vtmp2_ = AF_.take(512)
        X1L = p.sem(pref + "X1L")
        for c in range(NT):
            cs = slice(512 * c, 512 * (c + 1))
            yv = [p.dma("sync", yc[:, 512 * f:512 * (f + 1)],
                        ybzd[128 * f:128 * (f + 1), cs], LD) for f in range(NDF)]
            wls = [p.dma("gpsimd", oww[:, 1024 * kf:1024 * (kf + 1)],
                         outw_d[128 * kf:128 * (kf + 1), :], WD) for kf in range(NDF)]
            x2v = [p.dma("sync", x2c[:, 512 * f:512 * (f + 1)],
                         xT[128 * f:128 * (f + 1), cs], X1L) for f in range(NF)]
            p.wait("tensor", LD, yv[-1])
            p.wait("tensor", WD, wls[-1])
            oev = {0: 0, 1: 0}
            for dt_ in range(NF):
                par = dt_ % 2
                bank = pC if par == 0 else pD
                if oev[par]:
                    p.wait("tensor", EA, oev[par])
                mmv = None
                for kf in range(NDF):
                    mmv = p.op("tensor", lambda e, dt_=dt_, kf=kf, bank=bank: e.matmul(
                        bank[:], oww[:, 1024 * kf + 128 * dt_:1024 * kf + 128 * (dt_ + 1)],
                        yc[:, 512 * kf:512 * (kf + 1)],
                        start=(kf == 0), stop=(kf == NDF - 1)),
                        inc=(MM if kf == NDF - 1 else None))
                p.wait("scalar", MM, mmv)
                ev = p.op("scalar", lambda e, bank=bank, dt_=dt_: e.activation(
                    boc[:, 512 * dt_:512 * (dt_ + 1)], bank[:], AF.Copy), EA)
                oev[par] = ev
            p.wait("tensor", EA, ev)
            p.wait("scalar", EA, ev)
            r_done = ln_chunk(p, boc, sq3, meanb2, rstdb2, vtmp2_, conesf, epsc[:, 0:1], pA, pB, MM, EV, EA)
            p.wait("vector", EV, r_done)
            p.wait("vector", X1L, x2v[-1])
            od = None
            for f in range(NF):
                p.op("vector", lambda e, f=f: e.tensor_sub(
                    sq3[:, 512 * f:512 * (f + 1)], boc[:, 512 * f:512 * (f + 1)], meanb2[:]))
                p.op("vector", lambda e, f=f: e.tensor_mul(
                    sq3[:, 512 * f:512 * (f + 1)], sq3[:, 512 * f:512 * (f + 1)], rstdb2[:]))
                od = p.op("vector", lambda e, f=f: e.tensor_add(
                    x2c[:, 512 * f:512 * (f + 1)], x2c[:, 512 * f:512 * (f + 1)],
                    sq3[:, 512 * f:512 * (f + 1)]), EV)
            p.wait("sync", EV, od)
            for f in range(NF):
                p.dma("sync", x3T[128 * f:128 * (f + 1), cs], x2c[:, 512 * f:512 * (f + 1)], SO)
            p.barrier()
        p.wait("sync", SO, SO.val)
        p.barrier()
        B.AF.reset(afm0)
        B.AB.reset(abm0)


def build_l1_attn(T=4096):
    B = Bundle(T=T)
    xT = B.nc.dram_tensor("xT", [D, T], F32, kind="ExternalInput").ap()
    x3T = B.nc.dram_tensor("x3T", [D, T], F32, kind="ExternalOutput").ap()
    emit_l1_attn(B, T, "", xT, x3T)
    return B.finish()


def prep_l1_attn(inp, b, T=4096, x_in=None, pref="", with_consts=True):
    import ml_dtypes
    DI = 2048
    NDF = DI // 128
    NG = DI // 8
    d = {
        "xT": np.ascontiguousarray(x_in.T) if x_in is not None else None,
        "inw": np.ascontiguousarray(inp["L1_attn_in_proj_w"]),
        "xpw": np.ascontiguousarray(inp["L1_attn_x_proj_w"]),
        "dtw": np.ascontiguousarray(inp["L1_attn_dt_proj_w"]),
        "outw": np.ascontiguousarray(inp["L1_attn_out_proj_w"]),
    }
    cw = inp["L1_attn_conv_w"][:, 0, :]
    cwc = np.zeros((128, 4 * NDF), np.float32)
    for ft in range(NDF):
        for kk in range(4):
            cwc[:, 4 * ft + kk] = cw[kk, 128 * ft:128 * (ft + 1)]
    d["convw"] = cwc
    d["dtb"] = np.ascontiguousarray(inp["L1_attn_dt_proj_b"].reshape(NDF, 128).T)
    d["convb"] = np.ascontiguousarray(inp["L1_attn_conv_b"].reshape(NDF, 128).T)
    d["dcol"] = np.ascontiguousarray(inp["L1_attn_D"].reshape(NDF, 128).T)
    al = inp["L1_attn_A_log"]
    alc = np.zeros((128, NG), np.float32)
    for g in range(NG):
        alc[:, g] = al[8 * g:8 * (g + 1), :].reshape(128)
    d["alogc"] = alc
    selbc = np.zeros((32, 256), np.float32)
    for dn in range(128):
        selbc[dn % 16, dn] = 1.0
        selbc[16 + dn % 16, 128 + dn] = 1.0
    d["selbc"] = selbc
    rep8 = np.zeros((8, 128), np.float32)
    for dn in range(128):
        rep8[dn // 16, dn] = 1.0
    d["rep8"] = rep8
    d["rep8b"] = rep8.astype(ml_dtypes.bfloat16)
    rsel = np.zeros((128, 16 * 128), np.float32)
    for gi in range(16):
        for dn in range(128):
            rsel[dn, 128 * gi + 8 * gi + dn // 16] = 1.0
    d["rsel"] = rsel.astype(ml_dtypes.bfloat16)
    d = {pref + k: v for k, v in d.items() if v is not None}
    if with_consts:
        d.update(bundle_const_inputs())
    return d



def emit_l1_ffn(B, T, pref, xT, x4T, n_cores=8, CAP=384, debug=False):
    nc = B.nc
    p = B.p
    conesf = B.conesf
    epsc = B.epsc
    cid = B.cidb
    cidf = B.cidf
    afm0 = B.AF.mark()
    abm0 = B.AB.mark()
    NT = T // 512
    NB = T // 128
    E = 64
    K6 = 6
    NSLOT = E * CAP
    CAPF = CAP // 128
    BIG = float(NSLOT)

    sw1_d = nc.dram_tensor(pref + "sw1T", [D, 2048], F32, kind="ExternalInput").ap()
    sw2_d = nc.dram_tensor(pref + "sw2T", [2048, D], F32, kind="ExternalInput").ap()
    rw_d = nc.dram_tensor(pref + "rwT", [D, E], F32, kind="ExternalInput").ap()
    w1_d = nc.dram_tensor(pref + "w1", [E, D, 512], F32, kind="ExternalInput").ap()
    w2_d = nc.dram_tensor(pref + "w2", [E, 512, D], F32, kind="ExternalInput").ap()
    ebias_d = nc.dram_tensor(pref + "ebias", [1, E], F32, kind="ExternalInput").ap()
    rankm_d = nc.dram_tensor(pref + "rankm", [8, 1], F32, kind="ExternalInput").ap()
    tri_d = nc.dram_tensor(pref + "tri", [128, 128], F32, kind="ExternalInput").ap()
    iota64_d = nc.dram_tensor(pref + "iota64", [128, E], F32, kind="ExternalInput").ap()
    htokd = nc.dram_tensor(pref + "htokd", [T, D], BF16, kind="Internal").ap()
    sharedd = nc.dram_tensor(pref + "sharedd", [D, T], F32, kind="Internal").ap()
    xsd_t = nc.dram_tensor(pref + "xsd", [NSLOT + 1, D], BF16, kind="Internal")
    ysd_t = nc.dram_tensor(pref + "ysd", [NSLOT + 1, D], BF16, kind="Internal")
    xsd = xsd_t.ap()
    ysd = ysd_t.ap()
    expoutd = nc.dram_tensor(pref + "expoutd", [D, T], F32, kind="Internal").ap()
    ccin_t = nc.dram_tensor(pref + "ccin", [1, E], F32, kind="Internal")
    if debug:
        dbg_cnt = nc.dram_tensor("dbg_cnt", [1, E], F32, kind="ExternalOutput").ap()
        dbg_base = nc.dram_tensor("dbg_base", [1, E], F32, kind="ExternalOutput").ap()
        dbg_slot = nc.dram_tensor("dbg_slot", [128, T // 128 * 8], F32, kind="ExternalOutput").ap()
        dbg_pw = nc.dram_tensor("dbg_pw", [128, T // 128 * 8], F32, kind="ExternalOutput").ap()
        dbg_idx = nc.dram_tensor("dbg_idx", [128, T // 128 * 8], F32, kind="ExternalOutput").ap()
        dbg_pos = nc.dram_tensor("dbg_pos", [128, T // 128 * 8], F32, kind="ExternalOutput").ap()
        dbg_eo = nc.dram_tensor("dbg_eo", [D, T], F32, kind="ExternalOutput").ap()
        dbg_sh = nc.dram_tensor("dbg_sh", [D, T], F32, kind="ExternalOutput").ap()
        dbg_xs = nc.dram_tensor("dbg_xs", [NSLOT + 1, D], BF16, kind="ExternalOutput").ap()
        dbg_ys = nc.dram_tensor("dbg_ys", [NSLOT + 1, D], BF16, kind="ExternalOutput").ap()
    ccout_t = nc.dram_tensor(pref + "ccout", [n_cores, E], F32, kind="Internal")

    if True:
        meanb = B.sbuf(128, 512, F32)
        rstdb = B.sbuf(128, 512, F32)
        vtmp = B.sbuf(128, 512, F32)
        vtmp2 = B.sbuf(128, 128, F32)
        afm = B.AF.mark()
        abm = B.AB.mark()
        ctri = B.sbuf(128, 128, F32)
        ciota = B.sbuf(128, E, F32)
        ebrow = B.sbuf(1, E, F32)
        ebb = B.sbuf(128, E, F32)
        rankm = B.sbuf(8, 1, F32)
        cnt = B.sbuf(1, E, F32)
        baser = B.sbuf(1, E, F32)
        basee = B.sbuf(128, E, F32)
        cnts = B.sbuf(8, E, F32)
        m8 = B.sbuf(128, 8, F32)
        w6 = B.sbuf(128, 8, F32)
        scoresa = B.sbuf(128, NB * E, F32)
        idxall = B.sbuf(128, NB * 8, F32)
        pospick = B.sbuf(128, NB * 8, F32)
        pairw = B.sbuf(128, NB * 8, F32)
        idxu = B.ded(pref + "idxu", [128, 8], U32)
        slotint = B.ded(pref + "slotint", [128, NB * 8], I32)
        afmR = B.AF.mark()
        abmR = B.AB.mark()

        pA, pB, pC, pD = B.pA, B.pB, B.pC, B.pD
        pE, pF, pG, pT = B.pE, B.pF, B.pG, B.pT
        LD = p.sem(pref + "LD")
        WD = p.sem(pref + "WD")
        HS = p.sem(pref + "HS")
        SS = p.sem(pref + "SS")
        XS = p.sem(pref + "XS")
        YS = p.sem(pref + "YS")
        XL = p.sem(pref + "XL")
        SO = p.sem(pref + "SO")
        MM = p.sem(pref + "MM")
        EV = p.ms["vector"]
        EA = p.ms["scalar"]
        GP = p.ms["gpsimd"]
        AF_ = B.AF
        AB_ = B.AB

        p.dma("sync", ctri[:], tri_d, LD)
        p.dma("sync", ciota[:], iota64_d, LD)
        p.dma("sync", ebrow[:], ebias_d, LD)
        p.dma("sync", rankm[:], rankm_d, LD)
        p.op("vector", lambda e: e.memset(cnt[:], 0.0), EV)
        p.wait("tensor", LD, LD.val)
        bm = p.op("tensor", lambda e: e.matmul(pA[:, 0:E], conesf[0:1, :], ebrow[:],
                                               start=True, stop=True), inc=MM)
        p.wait("vector", MM, bm)
        p.op("vector", lambda e: e.tensor_copy(ebb[:], pA[:, 0:E]), EV)
        zt = AB_.take(1024)
        zm = p.op("vector", lambda e: e.memset(zt[:], 0.0), EV)
        p.wait("gpsimd", EV, zm)
        ZS = p.sem(pref + "ZS")
        for rb in range(NSLOT // 128):
            p.dma("gpsimd", xsd[128 * rb:128 * (rb + 1), :], zt[:], ZS)
        p.dma("gpsimd", xsd[NSLOT:NSLOT + 1, :], zt[0:1, :], ZS)
        p.dma("gpsimd", ysd[NSLOT:NSLOT + 1, :], zt[0:1, :], ZS)
        p.barrier()
        AB_.reset(abmR)

        xc = AF_.take(4096)
        hf32 = AF_.take(4096)
        sqa = AF_.take(4096)
        rwb = AF_.take(8 * E)
        hbf = AB_.take(4096)
        sw1b = AB_.take(8 * 2048)
        s1g = AB_.take(16 * 512)
        sw2b = AB_.take(16 * 1024)
        htst = AB_.take(2 * 128)
        rwl = [p.dma("sync", rwb[:, E * kf:E * (kf + 1)],
                     rw_d[128 * kf:128 * (kf + 1), :], LD) for kf in range(NF)]
        for c in range(NT):
            cs = slice(512 * c, 512 * (c + 1))
            lv = [p.dma("sync", xc[:, 512 * f:512 * (f + 1)],
                        xT[128 * f:128 * (f + 1), cs], LD) for f in range(NF)]
            w1l = [p.dma("gpsimd", sw1b[:, 2048 * kf:2048 * (kf + 1)],
                         sw1_d[128 * kf:128 * (kf + 1), :], WD) for kf in range(NF)]
            w2l = [p.dma("gpsimd", sw2b[:, 1024 * kf2:1024 * (kf2 + 1)],
                         sw2_d[128 * kf2:128 * (kf2 + 1), :], WD) for kf2 in range(16)]
            p.wait("scalar", LD, lv[-1])
            p.wait("tensor", LD, lv[-1])
            r_done = ln_chunk(p, xc, sqa, meanb, rstdb, vtmp, conesf, epsc[:, 0:1], pA, pB, MM, EV, EA)
            p.wait("vector", EV, r_done)
            for f in range(NF):
                p.op("vector", lambda e, f=f: e.tensor_sub(
                    sqa[:, 512 * f:512 * (f + 1)], xc[:, 512 * f:512 * (f + 1)], meanb[:]))
            a_done = None
            for f in range(NF):
                p.op("vector", lambda e, f=f: e.tensor_mul(
                    hf32[:, 512 * f:512 * (f + 1)], sqa[:, 512 * f:512 * (f + 1)], rstdb[:]))
                a_done = p.op("vector", lambda e, f=f: e.tensor_copy(
                    hbf[:, 512 * f:512 * (f + 1)], hf32[:, 512 * f:512 * (f + 1)]), EV)
            p.wait("tensor", EV, a_done)
            mmr = None
            for kf in range(NF):
                mmr = p.op("tensor", lambda e, kf=kf: e.matmul(
                    pC[0:E, :], rwb[:, E * kf:E * (kf + 1)], hf32[:, 512 * kf:512 * (kf + 1)],
                    start=(kf == 0), stop=(kf == NF - 1)), inc=(MM if kf == NF - 1 else None))
            p.wait("vector", MM, mmr)
            rle = p.op("vector", lambda e: e.tensor_copy(sqa[0:E, 0:512], pC[0:E, :]), EV)
            p.wait("tensor", EV, rle)
            sg = 0
            for j in range(4):
                bb = 4 * c + j
                if sg:
                    p.wait("tensor", EA, sg)
                tm = p.op("tensor", lambda e, j=j: e.transpose(
                    pD[:, 0:64], sqa[0:64, 128 * j:128 * (j + 1)], cidf[0:64, 0:64]), inc=MM)
                p.wait("scalar", MM, tm)
                sg = p.op("scalar", lambda e, bb=bb: e.activation(
                    scoresa[:, E * bb:E * (bb + 1)], pD[:, 0:64], AF.Sigmoid), EA)
            p.wait("tensor", WD, WD.val)
            tcp = 0
            for j in range(4):
                for f in range(NF):
                    if tcp:
                        p.wait("tensor", EV, tcp)
                    tm = p.op("tensor", lambda e, f=f, j=j: e.transpose(
                        pT[:], hbf[:, 512 * f + 128 * j:512 * f + 128 * (j + 1)],
                        cid[:]), inc=MM)
                    p.wait("vector", MM, tm)
                    p.wait("vector", HS, HS.val)
                    tcp = p.op("vector", lambda e, f=f: e.tensor_copy(
                        htst[:, 128 * (f % 2):128 * (f % 2) + 128], pT[:]), EV)
                    p.wait("sync", EV, tcp)
                    p.dma("sync", htokd[512 * c + 128 * j:512 * c + 128 * (j + 1),
                                        128 * f:128 * (f + 1)],
                          htst[:, 128 * (f % 2):128 * (f % 2) + 128], HS)
            sev = {0: 0, 1: 0}
            for et in range(16):
                par = et % 2
                bank = pE if par == 0 else pF
                if sev[par]:
                    p.wait("tensor", EA, sev[par])
                mmv = None
                for kf in range(NF):
                    mmv = p.op("tensor", lambda e, et=et, kf=kf, bank=bank: e.matmul(
                        bank[:], sw1b[:, 2048 * kf + 128 * et:2048 * kf + 128 * (et + 1)],
                        hbf[:, 512 * kf:512 * (kf + 1)],
                        start=(kf == 0), stop=(kf == NF - 1)),
                        inc=(MM if kf == NF - 1 else None))
                p.wait("scalar", MM, mmv)
                sev[par] = p.op("scalar", lambda e, bank=bank, et=et: e.activation(
                    s1g[:, 512 * et:512 * (et + 1)], bank[:], AF.Gelu_apprx_tanh), EA)
            p.wait("tensor", EA, sev[0])
            p.wait("tensor", EA, sev[1])
            shev = {0: 0, 1: 0}
            for dt_ in range(NF):
                par = dt_ % 2
                bank = pE if par == 0 else pF
                if shev[par]:
                    p.wait("tensor", EV, shev[par])
                mmv = None
                for kf2 in range(16):
                    mmv = p.op("tensor", lambda e, dt_=dt_, kf2=kf2, bank=bank: e.matmul(
                        bank[:], sw2b[:, 1024 * kf2 + 128 * dt_:1024 * kf2 + 128 * (dt_ + 1)],
                        s1g[:, 512 * kf2:512 * (kf2 + 1)],
                        start=(kf2 == 0), stop=(kf2 == 15)),
                        inc=(MM if kf2 == 15 else None))
                p.wait("vector", MM, mmv)
                p.wait("vector", SS, SS.val)
                shev[par] = p.op("vector", lambda e, bank=bank, dt_=dt_: e.tensor_copy(
                    sqa[:, 512 * dt_:512 * (dt_ + 1)], bank[:]), EV)
                p.wait("sync", EV, shev[par])
                p.dma("sync", sharedd[128 * dt_:128 * (dt_ + 1), cs],
                      sqa[:, 512 * dt_:512 * (dt_ + 1)], SS)
            p.barrier()
        AF_.reset(afmR); AB_.reset(abmR)

        posp = 0
        for b in range(NB):
            b8 = slice(8 * b, 8 * (b + 1))
            if posp:
                p.wait("tensor", EV, posp)
            p.op("vector", lambda e, b=b: e.tensor_add(vtmp[:, 0:64], scoresa[:, E * b:E * (b + 1)], ebb[:]))
            p.op("vector", lambda e: e.max(m8[:], vtmp[:, 0:64]))
            p.op("vector", lambda e: e.max_index(idxu[:], m8[:], vtmp[:, 0:64]))
            p.op("vector", lambda e, b8=b8: e.tensor_copy(idxall[:, b8], idxu[:]))
            for kk in range(K6):
                p.op("vector", lambda e, b=b, kk=kk: e.tensor_scalar(
                    vtmp[:, 0:64], ciota[:], idxall[:, 8 * b + kk:8 * b + kk + 1], None,
                    ALU.is_equal))
                p.op("vector", lambda e, b=b: e.tensor_mul(
                    vtmp[:, 64:128], vtmp[:, 0:64], scoresa[:, E * b:E * (b + 1)]))
                p.op("vector", lambda e, kk=kk: e.reduce_sum(w6[:, kk:kk + 1], vtmp[:, 64:128], AX))
            p.op("vector", lambda e: e.reduce_sum(vtmp[:, 0:1], w6[:, 0:K6], AX))
            p.op("vector", lambda e: e.tensor_scalar_add(vtmp[:, 0:1], vtmp[:, 0:1], 1e-20))
            p.op("vector", lambda e: e.reciprocal(vtmp[:, 1:2], vtmp[:, 0:1]))
            p.op("vector", lambda e, b=b: e.tensor_scalar_mul(
                pairw[:, 8 * b:8 * b + K6], w6[:, 0:K6], vtmp[:, 1:2]))
            p.op("vector", lambda e: e.memset(vtmp[:, 0:64], 0.0))
            hot = 0
            for kk in range(K6):
                p.op("vector", lambda e, b=b, kk=kk: e.tensor_scalar(
                    vtmp[:, 64:128], ciota[:], idxall[:, 8 * b + kk:8 * b + kk + 1], None,
                    ALU.is_equal))
                hot = p.op("vector", lambda e: e.tensor_add(vtmp[:, 0:64], vtmp[:, 0:64], vtmp[:, 64:128]), EV)
            p.wait("tensor", EV, hot)
            p.op("tensor", lambda e: e.matmul(pE[:, 0:64], ctri[:], vtmp[:, 0:64],
                                              start=True, stop=False))
            p.op("tensor", lambda e: e.matmul(pE[:, 0:64], conesf[0:1, :], cnt[:],
                                              start=False, stop=True))
            mcnt = p.op("tensor", lambda e: e.matmul(pF[0:1, 0:64], conesf[:, 0:1], vtmp[:, 0:64],
                                                     start=True, stop=True), inc=MM)
            p.wait("vector", MM, mcnt)
            p.op("vector", lambda e: e.tensor_add(cnt[:], cnt[:], pF[0:1, 0:64]))
            pp = 0
            for kk in range(K6):
                p.op("vector", lambda e, b=b, kk=kk: e.tensor_scalar(
                    vtmp[:, 64:128], ciota[:], idxall[:, 8 * b + kk:8 * b + kk + 1], None,
                    ALU.is_equal))
                p.op("vector", lambda e: e.tensor_mul(vtmp[:, 64:128], vtmp[:, 64:128], pE[:, 0:64]))
                pp = p.op("vector", lambda e, b=b, kk=kk: e.reduce_sum(
                    pospick[:, 8 * b + kk:8 * b + kk + 1], vtmp[:, 64:128], AX), EV)
            posp = pp
        p.barrier()

        p.dma("sync", ccin_t.ap(), cnt[:], LD)
        p.wait("gpsimd", LD, LD.val)
        cc = p.op("gpsimd", lambda e: e.collective_compute(
            "AllGather", ALU.bypass, replica_groups=[list(range(n_cores))],
            ins=[ccin_t.ap().opt()], outs=[ccout_t.ap().opt()]), GP)
        p.wait("sync", GP, cc)
        cl = p.dma("sync", cnts[0:n_cores, :], ccout_t.ap(), LD)
        p.wait("tensor", LD, cl)
        mb = p.op("tensor", lambda e: e.matmul(pF[0:1, 0:E], rankm[0:n_cores, :],
                                               cnts[0:n_cores, :], start=True, stop=True), inc=MM)
        p.wait("vector", MM, mb)
        be = p.op("vector", lambda e: e.tensor_copy(baser[:], pF[0:1, 0:E]), EV)
        p.wait("tensor", EV, be)
        mb2 = p.op("tensor", lambda e: e.matmul(pG[:, 0:E], conesf[0:1, :], baser[:],
                                                start=True, stop=True), inc=MM)
        p.wait("vector", MM, mb2)
        p.op("vector", lambda e: e.tensor_copy(basee[:], pG[:, 0:E]), EV)
        p.barrier()

        htk = AB_.take(1024)
        for b in range(NB):
            p.wait("sync", XS, XS.val)
            hlv = p.dma("sync", htk[:], htokd[128 * b:128 * (b + 1), :], XL)
            scv = 0
            for kk in range(K6):
                col = slice(8 * b + kk, 8 * b + kk + 1)
                p.op("vector", lambda e, col=col: e.tensor_scalar(
                    vtmp[:, 0:64], ciota[:], idxall[:, col], None, ALU.is_equal))
                p.op("vector", lambda e: e.tensor_mul(vtmp[:, 0:64], vtmp[:, 0:64], basee[:]))
                p.op("vector", lambda e: e.reduce_sum(vtmp[:, 64:65], vtmp[:, 0:64], AX))
                p.op("vector", lambda e, col=col: e.tensor_add(
                    vtmp[:, 65:66], vtmp[:, 64:65], pospick[:, col]))
                p.op("vector", lambda e: e.tensor_single_scalar(
                    vtmp[:, 66:67], vtmp[:, 65:66], float(CAP) - 0.5, ALU.is_lt))
                p.op("vector", lambda e, col=col: e.tensor_mul(
                    pairw[:, col], pairw[:, col], vtmp[:, 66:67]))
                p.op("vector", lambda e, col=col: e.tensor_scalar(
                    vtmp[:, 67:68], idxall[:, col], float(CAP), None, ALU.mult))
                p.op("vector", lambda e: e.tensor_add(vtmp[:, 67:68], vtmp[:, 67:68], vtmp[:, 65:66]))
                p.op("vector", lambda e: e.tensor_mul(vtmp[:, 67:68], vtmp[:, 67:68], vtmp[:, 66:67]))
                p.op("vector", lambda e: e.tensor_scalar(
                    vtmp[:, 68:69], vtmp[:, 66:67], -BIG, BIG, ALU.mult, ALU.add))
                p.op("vector", lambda e: e.tensor_add(vtmp[:, 67:68], vtmp[:, 67:68], vtmp[:, 68:69]))
                sv = p.op("vector", lambda e, col=col: e.tensor_copy(slotint[:, col], vtmp[:, 67:68]), EV)
                p.wait("gpsimd", EV, sv)
                p.wait("gpsimd", XL, hlv)
                vsc = XS.inc(16)
                p.ops["gpsimd"].append(
                    lambda e, col=col, s=XS.h: e.indirect_dma_start(
                        out=xsd, out_offset=IndirectOffsetOnAxis(slotint[:, col], 0),
                        in_=htk[:], in_offset=None).then_inc(s, 16))
        p.barrier()

        if debug:
            p.dma("sync", dbg_cnt, cnt[:], SO)
            p.dma("sync", dbg_base, baser[:], SO)
            p.dma("sync", dbg_pw, pairw[:], SO)
            p.dma("sync", dbg_idx, idxall[:], SO)
            p.dma("sync", dbg_pos, pospick[:], SO)
            dslot = AF_.take(NB * 8)
            dsv = p.op("vector", lambda e: e.tensor_copy(dslot, slotint[:]), EV)
            p.wait("sync", EV, dsv)
            p.dma("sync", dbg_slot, dslot, SO)
            p.barrier()

        xs = AB_.take(CAPF * 1024)
        xsT = AB_.take(8 * CAP)
        g1 = AB_.take(4 * CAP)
        w1b = AB_.take(8 * 512)
        w2b = AB_.take(4 * 1024)
        yst = AB_.take(2 * 1024)
        XSL = p.sem(pref + "XSL")
        WUL = p.sem(pref + "WUL")
        WNL = p.sem(pref + "WNL")
        last_e_mm = 0
        for ei in range(E):
            p.wait("sync", MM, last_e_mm)
            xlv = [p.dma("sync", xs[:, 1024 * s:1024 * (s + 1)],
                         xsd[CAP * ei + 128 * s:CAP * ei + 128 * (s + 1), :], XSL)
                   for s in range(CAPF)]
            p.wait("gpsimd", MM, last_e_mm)
            wul = [p.dma("gpsimd", w1b[:, 512 * kf:512 * (kf + 1)],
                         w1_d[ei, 128 * kf:128 * (kf + 1), :], WUL) for kf in range(NF)]
            wnl = [p.dma("gpsimd", w2b[:, 1024 * j:1024 * (j + 1)],
                         w2_d[ei, 128 * j:128 * (j + 1), :], WNL) for j in range(4)]
            p.wait("tensor", XSL, xlv[-1])
            tcp = 0
            for s in range(CAPF):
                for f in range(NF):
                    if tcp:
                        p.wait("tensor", EV, tcp)
                    tm = p.op("tensor", lambda e, s=s, f=f: e.transpose(
                        pT[:], xs[:, 1024 * s + 128 * f:1024 * s + 128 * (f + 1)], cid[:]), inc=MM)
                    p.wait("vector", MM, tm)
                    tcp = p.op("vector", lambda e, s=s, f=f: e.tensor_copy(
                        xsT[:, CAP * f + 128 * s:CAP * f + 128 * (s + 1)], pT[:]), EV)
            p.wait("tensor", EV, tcp)
            p.wait("tensor", WUL, wul[-1])
            gv = {0: 0, 1: 0}
            for jf in range(4):
                par = jf % 2
                bank = pA if par == 0 else pB
                if gv[par]:
                    p.wait("tensor", EA, gv[par])
                mmv = None
                for kf in range(NF):
                    mmv = p.op("tensor", lambda e, jf=jf, kf=kf, bank=bank: e.matmul(
                        bank[:, 0:CAP], w1b[:, 512 * kf + 128 * jf:512 * kf + 128 * (jf + 1)],
                        xsT[:, CAP * kf:CAP * (kf + 1)],
                        start=(kf == 0), stop=(kf == NF - 1)),
                        inc=(MM if kf == NF - 1 else None))
                p.wait("scalar", MM, mmv)
                gv[par] = p.op("scalar", lambda e, bank=bank, jf=jf: e.activation(
                    g1[:, CAP * jf:CAP * (jf + 1)], bank[:, 0:CAP], AF.Gelu_apprx_tanh), EA)
            p.wait("tensor", EA, gv[0])
            p.wait("tensor", EA, gv[1])
            p.wait("tensor", WNL, wnl[-1])
            yv = {0: 0, 1: 0}
            for s in range(CAPF):
                for fc in range(2):
                    par = fc
                    bank = pC if fc == 0 else pD
                    if yv[par]:
                        p.wait("tensor", EA, yv[par])
                    mmv = None
                    for jf in range(4):
                        mmv = p.op("tensor", lambda e, s=s, fc=fc, jf=jf, bank=bank: e.matmul(
                            bank[:], g1[:, CAP * jf + 128 * s:CAP * jf + 128 * (s + 1)],
                            w2b[:, 1024 * jf + 512 * fc:1024 * jf + 512 * (fc + 1)],
                            start=(jf == 0), stop=(jf == 3)),
                            inc=(MM if jf == 3 else None))
                    last_e_mm = mmv
                    p.wait("scalar", MM, mmv)
                    p.wait("scalar", YS, YS.val)
                    ye = p.op("scalar", lambda e, bank=bank, s=s, fc=fc: e.activation(
                        yst[:, 1024 * (s % 2) + 512 * fc:1024 * (s % 2) + 512 * (fc + 1)],
                        bank[:], AF.Copy), EA)
                    yv[par] = ye
                    p.wait("scalar", EA, ye)
                    p.dma("scalar", ysd[CAP * ei + 128 * s:CAP * ei + 128 * (s + 1),
                                        512 * fc:512 * (fc + 1)],
                          yst[:, 1024 * (s % 2) + 512 * fc:1024 * (s % 2) + 512 * (fc + 1)], YS)
        p.barrier()

        acc = AF_.take(2 * 1024)
        ghk = AB_.take(2 * 1024)
        p.op("vector", lambda e: e.memset(ghk[:], 0.0), EV)
        p.barrier()
        GDLp = [p.sem(pref + "GDL0"), p.sem(pref + "GDL1")]
        tcp = 0
        laststt = {0: 0, 1: 0}
        tmm = {0: 0, 1: 0}
        gkk = 0
        for b in range(NB):
            bpar = b % 2
            accb = acc[:, 1024 * bpar:1024 * (bpar + 1)]
            if tmm[bpar]:
                p.wait("vector", MM, tmm[bpar])
            p.op("vector", lambda e, accb=accb: e.memset(accb, 0.0), EV)
            for kk in range(K6):
                gpar2 = gkk % 2
                ghkb = ghk[:, 1024 * gpar2:1024 * (gpar2 + 1)]
                col = slice(8 * b + kk, 8 * b + kk + 1)
                if laststt[gpar2]:
                    p.wait("gpsimd", EV, laststt[gpar2])
                gd = GDLp[gpar2].inc(16)
                p.ops["gpsimd"].append(
                    lambda e, col=col, s=GDLp[gpar2].h, ghkb=ghkb: e.indirect_dma_start(
                        out=ghkb, out_offset=None,
                        in_=ysd, in_offset=IndirectOffsetOnAxis(slotint[:, col], 0)).then_inc(s, 16))
                p.wait("vector", GDLp[gpar2], gd)
                laststt[gpar2] = p.op("vector", lambda e, col=col, accb=accb, ghkb=ghkb: e.scalar_tensor_tensor(
                    accb, ghkb, pairw[:, col], accb, ALU.mult, ALU.add), EV)
                gkk += 1
            stt_b = laststt[(gkk - 1) % 2]
            for f in range(NF):
                if tcp:
                    p.wait("tensor", EV, tcp)
                p.wait("tensor", EV, stt_b)
                tm = p.op("tensor", lambda e, f=f, accb=accb: e.transpose(
                    pD[:, 0:128], accb[:, 128 * f:128 * (f + 1)], cidf[:]), inc=MM)
                tmm[bpar] = tm
                p.wait("vector", MM, tm)
                p.wait("vector", SO, SO.val)
                tcp = p.op("vector", lambda e: e.tensor_copy(vtmp2[:], pD[:, 0:128]), EV)
                p.wait("sync", EV, tcp)
                p.dma("sync", expoutd[128 * f:128 * (f + 1), 128 * b:128 * (b + 1)],
                      vtmp2[:], SO)
        p.barrier()
        AF_.reset(afm); AB_.reset(abm)

        if debug:
            p.dma("sync", dbg_eo, expoutd, SO)
            p.dma("sync", dbg_sh, sharedd, SO)
            p.dma("sync", dbg_xs, xsd, SO)
            p.dma("sync", dbg_ys, ysd, SO)
            p.barrier()

        eoc = AF_.take(4096)
        shc = AF_.take(4096)
        sq3 = AF_.take(4096)
        x3c = AF_.take(2048)
        X3L = p.sem(pref + "X3L")
        for c in range(NT):
            cs = slice(512 * c, 512 * (c + 1))
            evd = [p.dma("sync", eoc[:, 512 * f:512 * (f + 1)],
                         expoutd[128 * f:128 * (f + 1), cs], LD) for f in range(NF)]
            svd = [p.dma("sync", shc[:, 512 * f:512 * (f + 1)],
                         sharedd[128 * f:128 * (f + 1), cs], LD) for f in range(NF)]
            p.wait("vector", LD, svd[-1])
            sd = None
            for f in range(NF):
                sd = p.op("vector", lambda e, f=f: e.tensor_add(
                    eoc[:, 512 * f:512 * (f + 1)], eoc[:, 512 * f:512 * (f + 1)],
                    shc[:, 512 * f:512 * (f + 1)]), EV)
            p.wait("scalar", EV, sd)
            p.wait("tensor", EV, sd)
            r_done = ln_chunk(p, eoc, sq3, meanb, rstdb, vtmp, conesf, epsc[:, 0:1], pA, pB, MM, EV, EA)
            p.wait("vector", EV, r_done)
            for hf in range(2):
                p.wait("sync", SO, SO.val)
                x3v = [p.dma("sync", x3c[:, 512 * f:512 * (f + 1)],
                             xT[128 * (4 * hf + f):128 * (4 * hf + f) + 128, cs], X3L)
                       for f in range(4)]
                p.wait("vector", X3L, x3v[-1])
                od = None
                for f in range(4):
                    ff = 4 * hf + f
                    p.op("vector", lambda e, ff=ff: e.tensor_sub(
                        sq3[:, 512 * ff:512 * (ff + 1)], eoc[:, 512 * ff:512 * (ff + 1)], meanb[:]))
                    p.op("vector", lambda e, ff=ff: e.tensor_mul(
                        sq3[:, 512 * ff:512 * (ff + 1)], sq3[:, 512 * ff:512 * (ff + 1)], rstdb[:]))
                    od = p.op("vector", lambda e, ff=ff, f=f: e.tensor_add(
                        x3c[:, 512 * f:512 * (f + 1)], x3c[:, 512 * f:512 * (f + 1)],
                        sq3[:, 512 * ff:512 * (ff + 1)]), EV)
                p.wait("sync", EV, od)
                for f in range(4):
                    p.dma("sync", x4T[128 * (4 * hf + f):128 * (4 * hf + f) + 128, cs],
                          x3c[:, 512 * f:512 * (f + 1)], SO)
            p.barrier()
        p.wait("sync", SO, SO.val)
        p.barrier()
        B.AF.reset(afm0)
        B.AB.reset(abm0)


def build_l1_ffn(T=4096, n_cores=8, CAP=384, debug=False):
    B = Bundle(T=T)
    xT = B.nc.dram_tensor("xT", [D, T], F32, kind="ExternalInput").ap()
    x4T = B.nc.dram_tensor("x4T", [D, T], F32, kind="ExternalOutput").ap()
    emit_l1_ffn(B, T, "", xT, x4T, n_cores=n_cores, CAP=CAP, debug=debug)
    return B.finish()


def prep_l1_ffn(inp, b, T=4096, n_cores=8, x_in=None, pref="", with_consts=True):
    import ml_dtypes
    E = 64
    d = {
        "xT": np.ascontiguousarray(x_in.T) if x_in is not None else None,
        "sw1T": np.ascontiguousarray(inp["L1_ffn_sw1"].T),
        "sw2T": np.ascontiguousarray(inp["L1_ffn_sw2"].T),
        "rwT": np.ascontiguousarray(inp["L1_ffn_router_weight"].T),
        "w1": np.ascontiguousarray(inp["L1_ffn_w1"]),
        "w2": np.ascontiguousarray(inp["L1_ffn_w2"]),
        "ebias": np.ascontiguousarray(inp["L1_ffn_expert_bias"][None, :]),
    }
    rm = np.zeros((8, 1), np.float32)
    rm[:b, 0] = 1.0
    d["rankm"] = rm
    tri = (np.arange(128)[:, None] < np.arange(128)[None, :]).astype(np.float32)
    d["tri"] = tri
    d["iota64"] = np.tile(np.arange(E, dtype=np.float32)[None, :], (128, 1))
    d = {pref + k: v for k, v in d.items() if v is not None}
    if with_consts:
        d.update(bundle_const_inputs())
    return d


def prep_l0_attn(inp, b, T=4096, pref="", with_consts=True):
    H = 16
    slopes = np.array([2.0 ** (-0.5 * (h + 1)) for h in range(H)], np.float64)
    import ml_dtypes
    x = inp["L0_attn_x"][b]
    d = {
        "xT": np.ascontiguousarray(x.T),
        "WqT": np.ascontiguousarray(inp["L0_attn_Wq"].T),
        "WkT": np.ascontiguousarray(inp["L0_attn_Wk"].T),
        "WvT": np.ascontiguousarray(inp["L0_attn_Wv"].T),
        "WoT": np.ascontiguousarray(inp["L0_attn_Wo"].T),
        "bvrow": np.ascontiguousarray(inp["L0_attn_bv"][None, :]),
    }
    bcols = np.zeros((128, 32), np.float32)
    for j, nme in enumerate(("bq", "bk", "bo")):
        jj = (0, 1, 3)[j]
        v = inp["L0_attn_" + nme]
        bcols[:, 8 * jj:8 * (jj + 1)] = v.reshape(8, 128).T
    bcols[:, 16] = 1e-5
    d["bcols"] = bcols
    biask = np.zeros((66, 128), np.float32)
    biask[0, :] = 1.0
    biask[1, :] = np.arange(128)
    biask[64, :] = 1.0
    biask[65, :] = np.arange(128)
    d["biask"] = biask.astype(ml_dtypes.bfloat16)
    al = np.zeros((66, 8 * 512), np.float64)
    ll = np.arange(512, dtype=np.float64)
    for i in range(8):
        al[0, 512 * i:512 * (i + 1)] = -8.0 * slopes[2 * i] * ll
        al[1, 512 * i:512 * (i + 1)] = 8.0 * slopes[2 * i]
        al[64, 512 * i:512 * (i + 1)] = -8.0 * slopes[2 * i + 1] * ll
        al[65, 512 * i:512 * (i + 1)] = 8.0 * slopes[2 * i + 1]
    d["alibirhs"] = al.astype(ml_dtypes.bfloat16)
    cm = np.zeros((128, 4 * 512), np.float32)
    jj = np.arange(128)[:, None]
    for dlt in range(4):
        valid = ll[None, :] >= (128.0 * dlt + jj)
        cm[:, 512 * dlt:512 * (dlt + 1)] = np.where(valid, 0.0, NEG)
    d["cmask"] = cm.astype(ml_dtypes.bfloat16)
    bia = np.zeros((128, 576), np.float32)
    for h in range(16):
        for dlt in range(-3, 33):
            bia[:, 36 * h + dlt + 3] = -slopes[h] * 128.0 * dlt
    d["biacol"] = bia
    d = {pref + k: v for k, v in d.items()}
    if with_consts:
        d.update(bundle_const_inputs())
    return d




def prep_l0_ffn(inp, b, T=4096, x_in=None, pref="", with_consts=True):
    import ml_dtypes
    H, KV, DH = 16, 4, 64
    qkv_w = inp["L0_ffn_qkv_w"]
    qkv_b = inp["L0_ffn_qkv_b"]
    perm = []
    for pr in range(4):
        for u in range(4):
            h = 4 * pr + u
            perm += [64 * h + 2 * j for j in range(32)]
        for u in range(4):
            h = 4 * pr + u
            perm += [64 * h + 2 * j + 1 for j in range(32)]
    for g in range(4):
        perm += [1024 + 64 * g + 2 * j for j in range(32)]
    for g in range(4):
        perm += [1024 + 64 * g + 2 * j + 1 for j in range(32)]
    perm += list(range(1280, 1536))
    perm = np.array(perm)
    qkvw_p = np.ascontiguousarray(qkv_w[:, perm])
    qkvb_p = qkv_b[perm]
    d = {
        "xT": np.ascontiguousarray(x_in.T) if x_in is not None else None,
        "qkvw": qkvw_p,
        "projw": np.ascontiguousarray(inp["L0_ffn_proj_w"]),
        "gatew": np.ascontiguousarray(inp["L0_ffn_gate_w"]),
        "wup": np.ascontiguousarray(inp["L0_ffn_W_up"]),
        "wdn": np.ascontiguousarray(inp["L0_ffn_W_down"]),
        "vbrow": np.ascontiguousarray(qkvb_p[None, 1280:1536]),
        "gateb": np.ascontiguousarray(inp["L0_ffn_gate_b"][:, None]),
        "sinkr": np.ascontiguousarray(inp["L0_ffn_attn_sink"][None, :]),
    }
    d["qkvb"] = np.ascontiguousarray(qkvb_p[:1280].reshape(10, 128).T)
    ncols = np.zeros((128, 32), np.float32)
    ncols[:, 0:8] = inp["L0_ffn_attn_norm_w"].reshape(8, 128).T
    ncols[:, 8:16] = inp["L0_ffn_ffn_norm_w"].reshape(8, 128).T
    ncols[:, 16:24] = inp["L0_ffn_proj_b"].reshape(8, 128).T
    ncols[:, 24] = 1e-5
    d["ncols"] = ncols
    d["onesb"] = np.ones((1, 128), np.float32).astype(ml_dtypes.bfloat16)
    wm = np.zeros((128, 5 * 512), np.float32)
    jj = np.arange(128)[:, None]
    ll = np.arange(512)[None, :]
    for dlt in range(5):
        delta = 128 * (dlt - 1)
        qk = ll - jj - delta
        valid = (qk >= 0) & (qk <= 128)
        wm[:, 512 * dlt:512 * (dlt + 1)] = np.where(valid, 0.0, NEG)
    d["wmask"] = wm.astype(ml_dtypes.bfloat16)
    jjj = np.arange(32, dtype=np.float64)
    inv_freq = 150000.0 ** (-jjj / 32.0)
    pos = np.arange(T, dtype=np.float64) / 32.0
    ang = pos[None, :] * inv_freq[:, None]
    cos4 = np.tile(np.cos(ang), (4, 1))
    sin4 = np.tile(np.sin(ang), (4, 1))
    d["ctabq"] = (cos4 * 0.125).astype(ml_dtypes.bfloat16)
    d["stabq"] = (sin4 * 0.125).astype(ml_dtypes.bfloat16)
    d["ctabk"] = cos4.astype(ml_dtypes.bfloat16)
    d["stabk"] = sin4.astype(ml_dtypes.bfloat16)
    d["iota32"] = np.tile(np.arange(32, dtype=np.float32)[None, :], (128, 1))
    d = {pref + k: v for k, v in d.items() if v is not None}
    if with_consts:
        d.update(bundle_const_inputs())
    return d


def build_fused(T=4096, n_cores=8):
    B = Bundle(T=T)
    xT = B.nc.dram_tensor("xT", [D, T], F32, kind="ExternalInput").ap()
    x1Td = B.nc.dram_tensor("x1Td", [D, T], F32, kind="Internal").ap()
    x2Td = B.nc.dram_tensor("x2Td", [D, T], F32, kind="Internal").ap()
    x3Td = B.nc.dram_tensor("x3Td", [D, T], F32, kind="Internal").ap()
    x4T = B.nc.dram_tensor("x4T", [D, T], F32, kind="ExternalOutput").ap()
    emit_l0_attn(B, T, "a_", xT, x1Td)
    emit_l0_ffn(B, T, "f_", x1Td, x2Td)
    emit_l1_attn(B, T, "m_", x2Td, x3Td)
    emit_l1_ffn(B, T, "r_", x3Td, x4T, n_cores=n_cores)
    return B.finish()


def prep_fused(inputs, b, T=4096, n_cores=8):
    d = prep_l0_attn(inputs, b, T=T, pref="a_", with_consts=True)
    d["xT"] = d.pop("a_xT")
    d.update(prep_l0_ffn(inputs, b, T=T, x_in=None, pref="f_", with_consts=False))
    d.update(prep_l1_attn(inputs, b, T=T, x_in=None, pref="m_", with_consts=False))
    d.update(prep_l1_ffn(inputs, b, T=T, n_cores=n_cores, x_in=None, pref="r_",
                         with_consts=False))
    return d


def kernel(**inputs):
    T = 4096
    ncores = 8
    inputs = {k: np.asarray(v) for k, v in inputs.items()}
    nc = build_fused(T=T, n_cores=ncores)
    maps = [prep_fused(inputs, b, T=T, n_cores=ncores) for b in range(ncores)]
    res = run_bass_kernel_spmd(nc, maps, core_ids=list(range(ncores))).results
    return np.stack([np.ascontiguousarray(res[b]["x4T"].T) for b in range(ncores)])


def kernel_unfused(**inputs):
    T = 4096
    ncores = 8
    inputs = {k: np.asarray(v) for k, v in inputs.items()}

    nc1 = build_l0_attn(T=T)
    maps1 = [prep_l0_attn(inputs, b, T=T) for b in range(ncores)]
    r1 = run_bass_kernel_spmd(nc1, maps1, core_ids=list(range(ncores))).results

    nc2 = build_l0_ffn(T=T)
    maps2 = []
    for b in range(ncores):
        d = prep_l0_ffn(inputs, b, T=T, x_in=None)
        d["xT"] = r1[b]["x1T"]
        maps2.append(d)
    r2 = run_bass_kernel_spmd(nc2, maps2, core_ids=list(range(ncores))).results

    nc3 = build_l1_attn(T=T)
    maps3 = []
    for b in range(ncores):
        d = prep_l1_attn(inputs, b, T=T, x_in=None)
        d["xT"] = r2[b]["x2T"]
        maps3.append(d)
    r3 = run_bass_kernel_spmd(nc3, maps3, core_ids=list(range(ncores))).results

    nc4 = build_l1_ffn(T=T, n_cores=ncores)
    maps4 = []
    for b in range(ncores):
        d = prep_l1_ffn(inputs, b, T=T, n_cores=ncores, x_in=None)
        d["xT"] = r3[b]["x3T"]
        maps4.append(d)
    r4 = run_bass_kernel_spmd(nc4, maps4, core_ids=list(range(ncores))).results

    return np.stack([np.ascontiguousarray(r4[b]["x4T"].T) for b in range(ncores)])


if __name__ == "__main__":
    pass
